# version 12
# baseline (speedup 1.0000x reference)
import math
import numpy as np
import ml_dtypes
from contextlib import ExitStack

import concourse.bass as bass
import concourse.mybir as mybir
from concourse.bass_utils import run_bass_kernel_spmd

F32 = mybir.dt.float32
BF16 = mybir.dt.bfloat16
AF = mybir.ActivationFunctionType
ALU = mybir.AluOpType
AX = mybir.AxisListType

B, T, D = 2, 4096, 4096
NH, NKV, HD = 32, 8, 128
E_, DK, R_ = 8, 32, 16
TOK = 1024
NIC = D // 128
QH, KVH = 8, 2
SCALE = 1.0 / math.sqrt(HD)
SHIFT = 10.0
GSC = 1.0 / math.sqrt(DK)
EPS = 1e-5
NEG = -1.0e9

OF_WQG, OF_WKG, OF_AT, OF_BM, OF_GT = 0, 8192, 16384, 20480, 24576


class Ctr:
    def __init__(self, stack, nc, name):
        self.h = stack.enter_context(nc.semaphore(name))
        self.n = 0

    def inc(self, ins, k=1):
        ins.then_inc(self.h, k)
        self.n += k
        return self.n


class KB:
    def __init__(self, dump=(), stop="FFN"):
        self.dump = set(dump)
        self.stop = stop
        self.nc = bass.Bass("TRN2", target_bir_lowering=False)

    def _dma(self, ctr, eng):
        eng.wait_ge(ctr.h, ctr.n)
        return ctr

    def P(self, f): self.blk.tensor(f)
    def V(self, f): self.blk.vector(f)
    def S(self, f): self.blk.scalar(f)
    def G(self, f): self.blk.gpsimd(f)
    def Y(self, f): self.blk.sync(f)

    def build(self):
        nc = self.nc
        st = self.stack = ExitStack()
        st.__enter__()

        di = lambda n, shp, dt=F32: nc.dram_tensor(n, shp, dt, kind="ExternalInput").ap()
        self.x_in = di("x", [TOK, D])
        self.wq_dev = di("wqT", [QH, 128, NIC * 128])
        self.wk_dev = di("wkT", [KVH, 128, NIC * 128])
        self.wv_dev = di("wvT", [128, NIC * 256])
        self.wo_dev = di("woT", [8, 128, 8 * 512])
        self.wt_dev = di("wT", [8, 4, 128, 8 * 512])
        self.b_in = di("bffn", [1, D])
        self.wqg_dev = di("wqgT", [128, NIC * 256])
        self.wkg_dev = di("wkgT", [128, NIC * 256])
        self.at_dev = di("aT", [128, NIC * 128])
        self.bmT_in = di("bmT", [E_ * R_, D])
        self.cosT = di("cosT", [128, T])
        self.sinS = di("sinS", [128, T])
        self.masksT_in = di("masksT", [128, 4 * 512])
        self.ident_in = di("ident", [128, 128])

        self.out = nc.dram_tensor("out", [TOK, D], F32, kind="ExternalOutput").ap()

        dint = lambda n, shp, dt: nc.dram_tensor(n, shp, dt, kind="Internal").ap()
        self.xh_cc = [dint("xh_cc0", [D, TOK // 2], BF16),
                      dint("xh_cc1", [D, TOK // 2], BF16)]
        self.xh_all = [dint("xh_all0", [4 * D, TOK // 2], BF16),
                       dint("xh_all1", [4 * D, TOK // 2], BF16)]
        self.pw_dram = [dint(f"pw_dram{j}", [T, D // 4], BF16) for j in range(4)]
        self.x2a_dram = [dint(f"x2a_dram{j}", [TOK, D // 4], BF16) for j in range(4)]
        self.x2_dram = dint("x2_dram", [TOK, D], F32)

        self.dumps = {}
        for name, shp, dt in (
            ("d_xh", [4 * D, TOK], BF16),
            ("d_q", [128, QH * T], BF16),
            ("d_k", [128, KVH * T], BF16),
            ("d_v", [128, KVH * T], BF16),
            ("d_ao", [128, QH * T], BF16),
            ("d_x2", [TOK, D], F32),
            ("d_x2a", [TOK, D], BF16),
            ("d_st", [128, 64], F32),
        ):
            if name in self.dump:
                self.dumps[name] = nc.dram_tensor(name, shp, dt, kind="ExternalOutput").ap()

        sb = lambda n, shp, dt: st.enter_context(nc.sbuf_tensor(n, shp, dt))
        self.big0 = sb("big0", [128, 32768], BF16)
        self.big1 = sb("big1", [128, 32768], BF16)
        self.stg = sb("stg", [128, 2, 512], F32)
        self.ident = sb("ident_sb", [128, 128], F32)
        self.ident_bf = sb("ident_bf", [128, 128], BF16)
        self.ones_c = sb("ones_c", [128, 1], BF16)
        self.ones_r = sb("ones_r", [1, 128], F32)
        self.stat = sb("stat", [128, 16], F32)
        self.lrow = sb("lrow", [1, 512], F32)
        self.csts = sb("csts", [128, 2], F32)
        nc.const_aps.register(F32, EPS, self.csts[:, 0:1])
        nc.const_aps.register(F32, -SHIFT, self.csts[:, 1:2])

        ps = lambda n, shp: st.enter_context(nc.psum_tensor(n, shp, F32))
        self.sT2 = ps("sT2", [128, 1024])
        self.psA = self.sT2[:, 0:512]
        self.psB = self.sT2[:, 512:1024]
        self.pvA = ps("pvA", [128, 512])
        self.pvB = ps("pvB", [128, 512])
        self.psl = ps("psl", [128, 512])
        self.pg0 = ps("pg0", [128, 512])
        self.pg1 = ps("pg1", [128, 512])
        self.pg2 = ps("pg2", [128, 512])

        C = lambda n: Ctr(st, nc, n)
        self.pe, self.dv, self.ac = C("pe"), C("dv"), C("ac")
        self.g0, self.g1, self.g2, self.gi = C("g0"), C("g1"), C("g2"), C("gi")
        self.g3, self.g4 = C("g3"), C("g4")
        self.x0, self.x1, self.xq, self.xi = C("x0"), C("x1"), C("xq"), C("xi")
        self.so = C("so")
        self.cc1, self.cc2 = C("cc1"), C("cc2")

        self.blk = st.enter_context(nc.Block())

        self.emit_init()
        with nc.sbuf_tensor("xpair", [128, 3, 4096], F32) as xpair, \
             nc.sbuf_tensor("sqsink", [128, 512], F32) as sink:
            self.emit_ln_transpose(xpair, None, sink, src="x", dst_big=self.big0)
        self.emit_gather1()
        if self.stop == "A":
            self.emit_tail()
            st.__exit__(None, None, None)
            return self.nc
        with nc.sbuf_tensor("kT", [128, KVH * T], BF16) as kT_sb, \
             nc.sbuf_tensor("vsb", [128, KVH * T], BF16) as v_sb:
            with nc.sbuf_tensor("wslab", [128, 2, 4096], BF16) as wslab, \
                 nc.sbuf_tensor("wvsb", [128, NIC * 256], BF16) as wv_sb, \
                 nc.sbuf_tensor("cs", [128, 2, 512], F32) as cos_sb, \
                 nc.sbuf_tensor("sn", [128, 2, 512], F32) as sin_sb:
                self.emit_qkv(kT_sb, v_sb, wslab, wv_sb, cos_sb, sin_sb)
            if self.stop != "QKV":
                with nc.sbuf_tensor("masks", [128, 4 * 512], BF16) as mask_sb, \
                     nc.sbuf_tensor("ptc", [128, 4, 512], BF16) as ptc, \
                     nc.sbuf_tensor("rbst", [128, 512], F32) as rbst:
                    self.emit_attention(kT_sb, v_sb, mask_sb, ptc, rbst)
        if self.stop not in ("QKV", "ATT"):
            with nc.sbuf_tensor("woslab", [128, 2, 8, 512], BF16) as woslab, \
                 nc.sbuf_tensor("pwst", [128, 2, 512], BF16) as pwst:
                self.emit_wo_partial(woslab, pwst)
            self.emit_scatter2()
            with nc.sbuf_tensor("x2t", [128, 3, 4096], F32) as x2t, \
                 nc.sbuf_tensor("x2a", [128, 4096], BF16) as x2a, \
                 nc.sbuf_tensor("sqsink2", [128, 512], F32) as sink2:
                self.emit_ln_transpose(x2t, x2a, sink2, src="x2", dst_big=self.big1)
            if self.stop != "LN2":
                with nc.sbuf_tensor("wtslab", [128, 6, 8, 512], BF16) as wtslab, \
                     nc.sbuf_tensor("bb", [128, 4096], BF16) as bb, \
                     nc.sbuf_tensor("gbuf", [128, 5, 256], F32) as gbuf, \
                     nc.sbuf_tensor("x3st", [128, 2, 512], F32) as x3st:
                    self.emit_ffn(wtslab, bb, gbuf, self.stg, x3st)
        self.emit_tail()

        st.__exit__(None, None, None)
        return self.nc

    def emit_init(self):
        def g(g):
            self._dma(self.gi, g).inc(g.dma_start(out=self.ident[:], in_=self.ident_in), 16)
            self._dma(self.gi, g).inc(g.dma_start(out=self.ident_bf[:], in_=self.ident_in), 16)

        self.G(g)

        def v(v):
            v.memset(self.ones_c[:], 1.0)
            v.memset(self.csts[:, 0:1], EPS)
            v.memset(self.csts[:, 1:2], -SHIFT)
            self.dv.inc(v.memset(self.ones_r[:], 1.0))

        self.V(v)
        self.init_gi = self.gi.n

    def emit_ln_transpose(self, xpair, x2a, sink, src, dst_big):
        pe, dv, ac = self.pe, self.dv, self.ac
        x0, x1 = self.x0, self.x1
        bank_evac = {}
        slot_free = {}
        x2a_free = None

        for tt in range(8):
            slot = tt % 3
            xc = (x0, x1)[tt % 2]
            xtile = xpair[:, slot, :]

            def y(y, tt=tt, xc=xc, xtile=xtile, cons=slot_free.get(slot),
                  x2a_free=x2a_free):
                if src == "x2" and tt < 3:
                    y.wait_ge(self.cc2.h, 4)
                if cons is not None:
                    y.wait_ge(pe.h, cons)
                self._dma(xc, y).inc(y.dma_start(out=xtile,
                                   in_=self.x_in[tt * 128:(tt + 1) * 128, :]), 16)
                if src == "x2":
                    if x2a_free is not None:
                        y.wait_ge(dv.h, x2a_free)
                    for j in range(4):
                        self._dma(xc, y).inc(y.dma_start(
                            out=x2a[:, j * 1024:(j + 1) * 1024],
                            in_=self.x2a_dram[j][tt * 128:(tt + 1) * 128, :]), 16)

            self.Y(y)
            ldv = xc.n

            y2_so = None
            if src == "x2":
                def va(v, xtile=xtile, ldv=ldv, xc=xc):
                    v.wait_ge(xc.h, ldv)
                    dv.inc(v.tensor_tensor(xtile, xtile, x2a[:], op=ALU.add))

                self.V(va)
                dv_x2 = dv.n
                x2a_free = dv.n

                def y2(y, tt=tt, xtile=xtile, dv_x2=dv_x2):
                    y.wait_ge(dv.h, dv_x2)
                    self._dma(self.so, y).inc(y.dma_start(
                        out=self.x2_dram[tt * 128:(tt + 1) * 128, :], in_=xtile), 16)

                self.Y(y2)
                y2_so = self.so.n
                ready = ("dv", dv_x2)
            else:
                ready = ("xc", ldv)

            def v1(v, xtile=xtile, ready=ready, xc=xc):
                if ready[0] == "xc":
                    v.wait_ge(xc.h, ready[1])
                else:
                    v.wait_ge(dv.h, ready[1])
                dv.inc(v.reduce_sum(self.stat[:, 0:1], xtile, axis=AX.X))

            self.V(v1)
            dv_s1 = dv.n
            prev_var = getattr(self, "_lnt_prev_xhat", None)

            def s1(s, xtile=xtile, ready=ready, xc=xc, prev_var=prev_var):
                if ready[0] == "xc":
                    s.wait_ge(xc.h, ready[1])
                else:
                    s.wait_ge(dv.h, ready[1])
                if prev_var is not None:
                    s.wait_ge(dv.h, prev_var)
                for c in range(8):
                    ac.inc(s.activation(sink[:], xtile[:, c * 512:(c + 1) * 512],
                                        AF.Square, bias=0.0, scale=1.0,
                                        accum_out=self.stat[:, 8 + c:9 + c]))
                    if c < 7:
                        s.wait_ge(ac.h, ac.n)

            self.S(s1)
            ac_sq = ac.n

            def v2(v, ac_sq=ac_sq, dv_s1=dv_s1):
                v.wait_ge(ac.h, ac_sq)
                v.wait_ge(dv.h, dv_s1)
                v.reduce_sum(self.stat[:, 1:2], self.stat[:, 8:16], axis=AX.X)
                dv.inc(v.tensor_scalar_mul(self.stat[:, 2:3], self.stat[:, 0:1],
                                           -1.0 / D))
                v.wait_ge(dv.h, dv.n)
                dv.inc(v.tensor_mul(self.stat[:, 3:4], self.stat[:, 2:3],
                                    self.stat[:, 2:3]))
                v.wait_ge(dv.h, dv.n)
                dv.inc(v.tensor_scalar(self.stat[:, 4:5], self.stat[:, 1:2],
                                       1.0 / D, self.stat[:, 3:4],
                                       op0=ALU.mult, op1=ALU.subtract))

            self.V(v2)
            dv_var = dv.n

            def s2(s, dv_var=dv_var):
                s.wait_ge(dv.h, dv_var)
                ac.inc(s.activation(self.stat[:, 5:6], self.stat[:, 4:5], AF.Sqrt,
                                    bias=EPS, scale=1.0))

            self.S(s2)
            ac_std = ac.n

            def v3(v, ac_std=ac_std, xtile=xtile, y2_so=y2_so):
                v.wait_ge(ac.h, ac_std)
                if y2_so is not None:
                    v.wait_ge(self.so.h, y2_so)
                dv.inc(v.reciprocal(self.stat[:, 6:7], self.stat[:, 5:6]))
                v.wait_ge(dv.h, dv.n)
                dv.inc(v.tensor_scalar(xtile, xtile, self.stat[:, 2:3],
                                       self.stat[:, 6:7], op0=ALU.add, op1=ALU.mult))

            self.V(v3)
            dv_xhat = dv.n
            self._lnt_prev_xhat = dv_xhat

            for grp in range(8):
                par = grp % 2
                bank = (self.pg0, self.pg1)[par]

                def p(t, grp=grp, bank=bank, dv_xhat=dv_xhat, xtile=xtile,
                      prev=bank_evac.get(par), first=(tt == 0 and grp == 0)):
                    if first:
                        t.wait_ge(self.gi.h, self.init_gi)
                    t.wait_ge(dv.h, dv_xhat)
                    if prev is not None:
                        t.wait_ge(dv.h, prev)
                    for j in range(4):
                        ic = grp * 4 + j
                        ins = t.transpose(bank[:, j * 128:(j + 1) * 128],
                                          xtile[:, ic * 128:(ic + 1) * 128],
                                          self.ident[:])
                    pe.inc(ins)

                self.P(p)
                pe_tr = pe.n

                def v4(v, grp=grp, bank=bank, tt=tt, pe_tr=pe_tr):
                    v.wait_ge(pe.h, pe_tr)
                    o = dst_big[:].rearrange("p (ic t) -> p ic t", t=1024)
                    dv.inc(v.tensor_copy(
                        o[:, grp * 4:(grp + 1) * 4, tt * 128:(tt + 1) * 128],
                        bank[:].rearrange("p (j t) -> p j t", t=128)))

                self.V(v4)
                bank_evac[par] = dv.n
            slot_free[slot] = pe.n
            if src == "x" and tt == 3:
                self.ln1_half0_dv = self.dv.n
        if src == "x":
            self.ln1_done_dv = self.dv.n
        else:
            self.ln2_done_dv = self.dv.n
            self.x2_stores_so = self.so.n
            if "d_x2" in self.dumps:
                need = self.so.n

                def yd(y):
                    y.wait_ge(self.so.h, need)
                    self._dma(self.so, y).inc(y.dma_start(out=self.dumps["d_x2"],
                                            in_=self.x2_dram), 16)

                self.Y(yd)

    def emit_gather1(self):
        so = self.so
        needs = (self.ln1_half0_dv, self.ln1_done_dv)
        big = self.big0[:].rearrange("p (ic t) -> p ic t", t=1024)
        for hh in range(2):
            def y(y, hh=hh):
                y.wait_ge(self.dv.h, needs[hh])
                dst = self.xh_cc[hh].rearrange("(ic p) t -> p ic t", p=128)
                for h in range(4):
                    self._dma(so, y).inc(y.dma_start(
                        out=dst[:, h * 8:(h + 1) * 8, :],
                        in_=big[:, h * 8:(h + 1) * 8, hh * 512:(hh + 1) * 512]), 16)

            self.Y(y)
            need_so = so.n

            def g(g, hh=hh, need_so=need_so):
                g.wait_ge(so.h, need_so)
                self.cc1.inc(g.collective_compute(
                    "AllGather", ALU.bypass,
                    replica_groups=[[0, 1, 2, 3], [4, 5, 6, 7]],
                    ins=[self.xh_cc[hh]], outs=[self.xh_all[hh]]))

            self.G(g)

        if "d_xh" in self.dumps:
            def yd(y):
                y.wait_ge(self.cc1.h, 2)
                for hh in range(2):
                    self._dma(so, y).inc(y.dma_start(
                        out=self.dumps["d_xh"].rearrange(
                            "r (h t) -> r h t", h=2)[:, hh, :],
                        in_=self.xh_all[hh]), 16)
            self.Y(yd)

    def emit_qkv(self, kT_sb, v_sb, wslab, wv_sb, cos_sb, sin_sb):
        pe, dv, ac = self.pe, self.dv, self.ac
        g0, g1, xq, xi = self.g0, self.g1, self.xq, self.xi

        def gwv(g):
            g.wait_ge(dv.h, self.ln1_done_dv)
            self._dma(self.gi, g).inc(g.dma_start(out=wv_sb[:], in_=self.wv_dev), 16)

        self.G(gwv)
        gi_wv = self.gi.n

        qk_banks = (self.psA, self.psB, self.pg0[:], self.pg1[:])
        bank_evac = {}
        slab_cons = {}
        self._qkv_sli = 0
        self._qkv_bc = 0
        xq_cons = None
        rope_done = None

        for qr in range(4):
            xq_ld_h = [None, None]
            for hh in range(2):
                def yq(y, qr=qr, hh=hh, xq_cons=xq_cons):
                    if qr == 0:
                        y.wait_ge(self.cc1.h, hh + 1)
                    elif hh == 0:
                        y.wait_ge(pe.h, xq_cons)
                    dst = self.big0[:].rearrange("p (ic t) -> p ic t", t=1024)
                    src = self.xh_all[hh][qr * D:(qr + 1) * D, :].rearrange(
                        "(ic p) t -> p ic t", p=128)
                    for h in range(2):
                        self._dma(xq, y).inc(y.dma_start(
                            out=dst[:, h * 16:(h + 1) * 16,
                                    hh * 512:(hh + 1) * 512],
                            in_=src[:, h * 16:(h + 1) * 16, :]), 16)

                self.Y(yq)
                xq_ld_h[hh] = xq.n
            xq_ld = xq.n
            big = self.big0[:].rearrange("p (ic t) -> p ic t", t=1024)

            def ycs(y, qr=qr, rope_done=rope_done):
                y.wait_ge(dv.h, self.ln1_done_dv if rope_done is None else rope_done)
                for tb in range(2):
                    c0 = qr * 1024 + tb * 512
                    self._dma(xi, y).inc(y.dma_start(out=cos_sb[:, tb, :],
                                       in_=self.cosT[:, c0:c0 + 512]), 16)
                    self._dma(xi, y).inc(y.dma_start(out=sin_sb[:, tb, :],
                                       in_=self.sinS[:, c0:c0 + 512]), 16)

            self.Y(ycs)
            xi_cs = xi.n

            work = ([(oc, (0, 1)) for oc in range(QH + KVH)] if qr > 0 else
                    [(oc, (0,)) for oc in range(QH + KVH)] +
                    [(oc, (1,)) for oc in range(QH + KVH)])
            for oc, tbs in work:
                is_q = oc < QH
                wsrc = self.wq_dev[oc] if is_q else self.wk_dev[oc - QH]
                slot = self._qkv_sli % 2
                self._qkv_sli += 1
                gc = (g0, g1)[slot]

                def gw(g, wsrc=wsrc, slot=slot, gc=gc,
                       cons=slab_cons.get(slot)):
                    if cons is not None:
                        g.wait_ge(pe.h, cons)
                    else:
                        g.wait_ge(dv.h, self.ln1_done_dv)
                    self._dma(gc, g).inc(g.dma_start(out=wslab[:, slot, :], in_=wsrc), 16)

                self.G(gw)
                gw_ld = gc.n

                for tb in tbs:
                    bidx = self._qkv_bc % 4
                    self._qkv_bc += 1
                    bank = qk_banks[bidx]

                    def p(t, tb=tb, bank=bank, slot=slot, gw_ld=gw_ld, gc=gc,
                          xq_ld=xq_ld_h[tb], ev=bank_evac.get(bidx)):
                        t.wait_ge(gc.h, gw_ld)
                        t.wait_ge(xq.h, xq_ld)
                        if ev is not None:
                            t.wait_ge(dv.h, ev)
                        for ic in range(NIC):
                            ins = t.matmul(bank,
                                           wslab[:, slot, ic * 128:(ic + 1) * 128],
                                           big[:, ic, tb * 512:(tb + 1) * 512],
                                           start=(ic == 0), stop=(ic == NIC - 1))
                        pe.inc(ins)

                    self.P(p)
                    pe_mm = pe.n
                    slab_cons[slot] = pe.n

                    def vr(v, oc=oc, tb=tb, qr=qr, bank=bank, pe_mm=pe_mm,
                           is_q=is_q, xi_cs=xi_cs, dvfence=dv.n):
                        v.wait_ge(pe.h, pe_mm)
                        v.wait_ge(xi.h, xi_cs)
                        v.wait_ge(dv.h, dvfence)
                        tmp = self.stg[:, 0, :]
                        t2 = self.stg[:, 1, :]
                        sn = sin_sb[:, tb, :]
                        cs = cos_sb[:, tb, :]
                        v.tensor_tensor(tmp[0:64, :], bank[64:128, :], sn[0:64, :],
                                        op=ALU.mult)
                        v.tensor_tensor(tmp[64:128, :], bank[0:64, :], sn[64:128, :],
                                        op=ALU.mult)
                        dv.inc(v.tensor_tensor(t2[:], bank, cs, op=ALU.mult))
                        v.wait_ge(dv.h, dv.n)
                        s0 = (oc * T if is_q else (oc - QH) * T) + qr * 1024 + tb * 512
                        dst = (self.big1 if is_q else kT_sb)[:, s0:s0 + 512]
                        dv.inc(v.tensor_tensor(dst, t2[:], tmp[:], op=ALU.add))

                    self.V(vr)
                    bank_evac[bidx] = dv.n

            rope_done = dv.n

            for tt in range(8):
                cg = qr * 8 + tt

                def pv(t, tt=tt, gi_wv=gi_wv, xq_ld=xq_ld_h[tt // 4],
                       ev=bank_evac.get("v")):
                    t.wait_ge(self.gi.h, gi_wv)
                    t.wait_ge(xq.h, xq_ld)
                    if ev is not None:
                        t.wait_ge(dv.h, ev)
                    for ic in range(NIC):
                        ins = t.matmul(self.pg2[:, 0:256],
                                       big[:, ic, tt * 128:(tt + 1) * 128],
                                       wv_sb[:, ic * 256:(ic + 1) * 256],
                                       start=(ic == 0), stop=(ic == NIC - 1))
                    pe.inc(ins)

                self.P(pv)
                pe_v = pe.n
                xq_cons = pe.n

                def vv(v, cg=cg, pe_v=pe_v):
                    v.wait_ge(pe.h, pe_v)
                    dst = v_sb[:].rearrange("p (kv c d) -> p kv c d", kv=KVH, d=128)
                    src = self.pg2[:, 0:256].rearrange("p (kv d) -> p kv d", d=128)
                    dv.inc(v.tensor_copy(dst[:, :, cg, :], src))

                self.V(vv)
                bank_evac["v"] = dv.n

        self.qkv_done_dv = self.dv.n

        for nm, buf in (("d_q", self.big1[:, 0:QH * T]), ("d_k", kT_sb[:]),
                        ("d_v", v_sb[:])):
            if nm in self.dumps:
                def yd(y, nm=nm, buf=buf):
                    y.wait_ge(dv.h, self.qkv_done_dv)
                    self._dma(self.so, y).inc(y.dma_start(out=self.dumps[nm], in_=buf), 16)
                self.Y(yd)

    def emit_attention(self, kT_sb, v_sb, mask_sb, ptc, rbst):
        pe, dv, ac = self.pe, self.dv, self.ac

        def gm(g):
            g.wait_ge(self.dv.h, self.qkv_done_dv)
            self._dma(self.gi, g).inc(g.dma_start(out=mask_sb[:], in_=self.masksT_in), 16)

        self.G(gm)
        gi_m = self.gi.n
        qkv_dv = self.qkv_done_dv

        sT_banks = (self.psA, self.psB)
        pv_banks = (self.pvA, self.pvB)
        l_banks = (self.psl, self.pg2)
        LROW = (0, 32, 64, 0)

        chunk_list = []
        groups = []
        for kvh in range(KVH):
            for qh4 in range(4):
                qh = kvh * 4 + qh4
                for qc in range(8):
                    g = len(groups)
                    n = 4 * (qc + 1)
                    for ci in range(n):
                        bk, cc = ci // 4, ci % 4
                        chunk_list.append((g, kvh, qh, qc, bk * 4 + cc,
                                           cc if bk == qc else None,
                                           ci == 0, ci == n - 1))
                    groups.append((kvh, qh, qc))
        NCH = len(chunk_list)
        NG = len(groups)

        ac0 = ac.n
        exp_ac = lambda k: ac0 + k + 1
        dv0 = dv.n
        vf_dv = lambda g: dv0 + 6 * g + 3
        rbcopy_dv = lambda g: dv0 + 6 * g + 4
        aomult_dv = lambda g: dv0 + 6 * g + 6

        pe_sT = [None] * NCH
        pe_slotfree = [None] * NCH
        pe_rb = [None] * NG
        pe_pv = [None] * NG
        first_k = {}
        for k, (g, *_rest) in enumerate(chunk_list):
            first_k.setdefault(g, k)

        def P(t):
            pending_rb = []

            def emit_sT(k):
                g, kvh, qh, qc, kc, diag_cc, first, last = chunk_list[k]
                if k == 0:
                    t.wait_ge(dv.h, qkv_dv)
                    t.wait_ge(self.gi.h, gi_m)
                if k >= 2:
                    t.wait_ge(ac.h, exp_ac(k - 2))
                if first and g >= 2:
                    t.wait_ge(dv.h, aomult_dv(g - 2))
                ins = t.matmul(
                    sT_banks[k % 2],
                    kT_sb[:, kvh * T + kc * 128:kvh * T + (kc + 1) * 128],
                    self.big1[:, qh * T + qc * 512:qh * T + qc * 512 + 512],
                    start=True, stop=(diag_cc is None))
                if diag_cc is not None:
                    ins = t.matmul(sT_banks[k % 2], self.ident_bf[:],
                                   mask_sb[:, diag_cc * 512:(diag_cc + 1) * 512],
                                   start=False, stop=True)
                pe.inc(ins)
                pe_sT[k] = pe.n

            def emit_p2(k):
                g, kvh, qh, qc, kc, diag_cc, first, last = chunk_list[k]
                ci = k - first_k[g]
                t.wait_ge(ac.h, exp_ac(k))
                pe.inc(t.matmul(
                    pv_banks[g % 2][:],
                    v_sb[:, kvh * T + kc * 128:kvh * T + (kc + 1) * 128],
                    ptc[:, k % 4, :], start=first, stop=last))
                pe_slotfree[k] = pe.n
                if ci % 4 == 3:
                    nb = qc + 1
                    b = ci // 4
                    lb = l_banks[g % 2]
                    for j in range(4):
                        kk = k - 3 + j
                        row = LROW[j]
                        is_start = (b == 0) and (j != 3)
                        is_stop = (b == nb - 1) and (j != 0)
                        pe.inc(t.matmul(
                            lb[row:row + 1, :], self.ones_c[:],
                            ptc[:, kk % 4, :],
                            start=is_start, stop=is_stop,
                            tile_position=(0, row)))
                        pe_slotfree[kk] = pe.n
                if last:
                    pe_pv[g] = pe.n
                    pending_rb.append((g, k + 4))

            def emit_rb(g):
                t.wait_ge(dv.h, vf_dv(g))
                if g >= 2:
                    t.wait_ge(dv.h, rbcopy_dv(g - 2))
                rb = (self.pg0, self.pg1)[g % 2]
                pe.inc(t.matmul(rb[:], self.ones_r[:], self.lrow[:],
                                start=True, stop=True))
                pe_rb[g] = pe.n

            for k in range(NCH):
                emit_sT(k)
                while pending_rb and pending_rb[0][1] <= k:
                    emit_rb(pending_rb.pop(0)[0])
                if k >= 2:
                    emit_p2(k - 2)
            emit_p2(NCH - 2)
            emit_p2(NCH - 1)
            while pending_rb:
                emit_rb(pending_rb.pop(0)[0])

        self.P(P)

        def V(v):
            for g, (kvh, qh, qc) in enumerate(groups):
                lb = l_banks[g % 2]
                v.wait_ge(pe.h, pe_pv[g])
                v.wait_ge(dv.h, dv.n)
                dv.inc(v.tensor_copy(rbst[0:1, :], lb[0:1, :]))
                v.wait_ge(dv.h, dv.n)
                dv.inc(v.tensor_tensor(rbst[0:1, :], rbst[0:1, :], lb[32:33, :],
                                       op=ALU.add))
                v.wait_ge(dv.h, dv.n)
                dv.inc(v.tensor_tensor(self.lrow[:], rbst[0:1, :], lb[64:65, :],
                                       op=ALU.add))
                assert dv.n == vf_dv(g)
                v.wait_ge(pe.h, pe_rb[g])
                v.wait_ge(dv.h, dv.n)
                rb = (self.pg0, self.pg1)[g % 2]
                dv.inc(v.tensor_copy(rbst[:], rb[:]))
                assert dv.n == rbcopy_dv(g)
                v.wait_ge(dv.h, dv.n)
                dv.inc(v.reciprocal(rbst[:], rbst[:]))
                v.wait_ge(dv.h, dv.n)
                dst = self.big0[:, qh * T + qc * 512:qh * T + qc * 512 + 512]
                dv.inc(v.tensor_tensor(dst, pv_banks[g % 2][:], rbst[:],
                                       op=ALU.mult))
                assert dv.n == aomult_dv(g)

        self.V(V)

        def S(s):
            for k in range(NCH):
                s.wait_ge(pe.h, pe_sT[k])
                if k >= 4:
                    s.wait_ge(pe.h, pe_slotfree[k - 4])
                ac.inc(s.activation(ptc[:, k % 4, :], sT_banks[k % 2],
                                    AF.Exp, bias=-SHIFT, scale=SCALE))
                assert ac.n == exp_ac(k)

        self.S(S)

        self.attn_done_dv = self.dv.n
        if "d_ao" in self.dumps:
            def yd(y):
                y.wait_ge(dv.h, self.attn_done_dv)
                self._dma(self.so, y).inc(y.dma_start(out=self.dumps["d_ao"],
                                        in_=self.big0[:, 0:QH * T]), 16)
            self.Y(yd)

    def emit_wo_partial(self, woslab, pwst):
        pe, dv = self.pe, self.dv
        g0, g1, x0, x1 = self.g0, self.g1, self.x0, self.x1
        attn_dv = self.attn_done_dv
        slab_cons = {}
        bank_evac = {}
        store_state = {}
        it = 0

        for fb in range(8):
            slot = fb % 2
            gc = (g0, g1)[slot]

            def gw(g, fb=fb, slot=slot, gc=gc, cons=slab_cons.get(slot)):
                if cons is not None:
                    g.wait_ge(pe.h, cons)
                else:
                    g.wait_ge(dv.h, attn_dv)
                self._dma(gc, g).inc(g.dma_start(
                    out=woslab[:, slot, :, :].rearrange("p a b -> p (a b)"),
                    in_=self.wo_dev[fb]), 16)

            self.G(gw)
            gw_ld = gc.n

            for tt in range(32):
                par = it % 2
                bank = (self.psA, self.psB)[par]

                def p(t, tt=tt, slot=slot, bank=bank, gw_ld=gw_ld, gc=gc,
                      first=(it == 0), ev=bank_evac.get(par)):
                    if first:
                        t.wait_ge(dv.h, attn_dv)
                    t.wait_ge(gc.h, gw_ld)
                    if ev is not None:
                        t.wait_ge(dv.h, ev)
                    for qh in range(QH):
                        ins = t.matmul(
                            bank[:],
                            self.big0[:, qh * T + tt * 128:qh * T + (tt + 1) * 128],
                            woslab[:, slot, qh, :],
                            start=(qh == 0), stop=(qh == QH - 1))
                    pe.inc(ins)

                self.P(p)
                pe_mm = pe.n
                slab_cons[slot] = pe.n

                def ve(v, par=par, bank=bank, pe_mm=pe_mm,
                       stfree=store_state.get(par)):
                    v.wait_ge(pe.h, pe_mm)
                    if stfree is not None:
                        v.wait_ge((x0, x1)[par].h, stfree)
                    dv.inc(v.tensor_copy(pwst[:, par, :], bank[:]))

                self.V(ve)
                bank_evac[par] = dv.n
                dv_e = dv.n

                def ys(y, fb=fb, tt=tt, par=par, dv_e=dv_e):
                    y.wait_ge(dv.h, dv_e)
                    self._dma((x0, x1)[par], y).inc(y.dma_start(
                        out=self.pw_dram[fb // 2][tt * 128:(tt + 1) * 128,
                                                  (fb % 2) * 512:(fb % 2) * 512 + 512],
                        in_=pwst[:, par, :]), 16)

                self.Y(ys)
                store_state[par] = (x0, x1)[par].n
                it += 1

            if fb % 2 == 1:
                j = fb // 2
                sx0, sx1 = self.x0.n, self.x1.n

                def g(g, j=j, sx0=sx0, sx1=sx1):
                    g.wait_ge(self.x0.h, sx0)
                    g.wait_ge(self.x1.h, sx1)
                    self.cc2.inc(g.collective_compute(
                        "ReduceScatter", ALU.add,
                        replica_groups=[[0, 1, 2, 3], [4, 5, 6, 7]],
                        ins=[self.pw_dram[j]], outs=[self.x2a_dram[j]]))

                self.G(g)

        self.wo_done_pe = self.pe.n
        self.d0_done_dv = self.dv.n

    def emit_scatter2(self):
        if "d_x2a" in self.dumps:
            def yd(y):
                y.wait_ge(self.cc2.h, 4)
                for j in range(4):
                    self._dma(self.so, y).inc(y.dma_start(
                        out=self.dumps["d_x2a"][:, j * 1024:(j + 1) * 1024],
                        in_=self.x2a_dram[j]), 16)
            self.Y(yd)

    def emit_ffn(self, wtslab, bb, gbuf, xst, x3st):
        pe, dv, ac = self.pe, self.dv, self.ac
        g0, g1, x0, x1, xi = self.g0, self.g1, self.x0, self.x1, self.xi
        ln2_dv = self.ln2_done_dv
        x2_so = self.x2_stores_so
        wo_pe = self.wo_done_pe
        big0, big1 = self.big0, self.big1

        def gco(g):
            g.wait_ge(pe.h, wo_pe)
            self._dma(self.gi, g).inc(g.dma_start(out=big0[:, OF_WQG:OF_WQG + 8192],
                                    in_=self.wqg_dev), 16)
            self._dma(self.gi, g).inc(g.dma_start(out=big0[:, OF_WKG:OF_WKG + 8192],
                                    in_=self.wkg_dev), 16)
            self._dma(self.gi, g).inc(g.dma_start(out=big0[:, OF_AT:OF_AT + 4096],
                                    in_=self.at_dev), 16)
            self._dma(self.gi, g).inc(g.dma_start(out=big0[:, OF_BM:OF_BM + 4096],
                                    in_=self.bmT_in), 16)

        self.G(gco)
        gi_c = self.gi.n

        def gbb(g):
            g.wait_ge(dv.h, ln2_dv)
            src = self.b_in.broadcast_to([1, D, 128]).transpose([2, 0, 1]) \
                .rearrange("p one d -> p (one d)")
            self._dma(self.gi, g).inc(g.dma_start(out=bb[:], in_=src), 16)

        self.G(gbb)
        gi_bb = self.gi.n

        xh2 = big1[:].rearrange("p (ic t) -> p ic t", t=1024)

        gate_prev = None
        gt_prev = None
        for tt in range(8):
            def pq(t, tt=tt, first=(tt == 0), prev=gate_prev):
                if first:
                    t.wait_ge(self.gi.h, gi_c)
                    t.wait_ge(dv.h, ln2_dv)
                if prev is not None:
                    t.wait_ge(dv.h, prev)
                for ic in range(NIC):
                    t.matmul(self.pg0[:, 0:256],
                             xh2[:, ic, tt * 128:(tt + 1) * 128],
                             big0[:, OF_WQG + ic * 256:OF_WQG + (ic + 1) * 256],
                             start=(ic == 0), stop=(ic == NIC - 1))
                for ic in range(NIC):
                    ins = t.matmul(self.pg1[:, 0:256],
                                   xh2[:, ic, tt * 128:(tt + 1) * 128],
                                   big0[:, OF_WKG + ic * 256:OF_WKG + (ic + 1) * 256],
                                   start=(ic == 0), stop=(ic == NIC - 1))
                pe.inc(ins)

            self.P(pq)
            pe_qk = pe.n

            def vg(v, pe_qk=pe_qk):
                v.wait_ge(pe.h, pe_qk)
                dv.inc(v.tensor_copy(gbuf[:, 0, :], self.pg0[:, 0:256]))
                v.wait_ge(dv.h, dv.n)
                dv.inc(v.tensor_tensor(gbuf[:, 1, :], self.pg1[:, 0:256],
                                       gbuf[:, 0, :], op=ALU.mult))
                v.wait_ge(dv.h, dv.n)
                dv.inc(v.tensor_reduce(gbuf[:, 2, 0:8],
                                       gbuf[:, 1, :].rearrange("p (e d) -> p e d",
                                                               d=DK),
                                       axis=AX.X, op=ALU.add))
                v.wait_ge(dv.h, dv.n)
                dv.inc(v.reduce_max(self.stat[:, 8:9], gbuf[:, 2, 0:8], axis=AX.X,
                                    negate=True))
                v.wait_ge(dv.h, dv.n)
                dv.inc(v.tensor_scalar_mul(self.stat[:, 9:10], self.stat[:, 8:9],
                                           GSC))

            self.V(vg)
            dv_sc = dv.n

            def sg(s, dv_sc=dv_sc):
                s.wait_ge(dv.h, dv_sc)
                ac.inc(s.activation(gbuf[:, 3, 0:8], gbuf[:, 2, 0:8], AF.Exp,
                                    bias=self.stat[:, 9:10], scale=GSC,
                                    accum_out=self.stat[:, 10:11]))

            self.S(sg)
            ac_e = ac.n

            def vg2(v, ac_e=ac_e):
                v.wait_ge(ac.h, ac_e)
                dv.inc(v.reciprocal(self.stat[:, 11:12], self.stat[:, 10:11]))
                v.wait_ge(dv.h, dv.n)
                dv.inc(v.tensor_scalar(gbuf[:, 4, 0:8], gbuf[:, 3, 0:8],
                                       self.stat[:, 11:12], None, op0=ALU.mult))

            self.V(vg2)
            gate_prev = dv.n

            def ph(t, tt=tt, first=(tt == 0)):
                if first:
                    t.wait_ge(dv.h, ln2_dv)
                for ic in range(NIC):
                    ins = t.matmul(self.pg2[:, 0:128],
                                   xh2[:, ic, tt * 128:(tt + 1) * 128],
                                   big0[:, OF_AT + ic * 128:OF_AT + (ic + 1) * 128],
                                   start=(ic == 0), stop=(ic == NIC - 1))
                pe.inc(ins)

            self.P(ph)
            pe_h = pe.n

            def vh(v, pe_h=pe_h, wg_dv=gate_prev):
                v.wait_ge(pe.h, pe_h)
                v.wait_ge(dv.h, wg_dv)
                wv = gbuf[:, 4, 0:8]
                dv.inc(v.tensor_tensor(
                    gbuf[:, 1, 0:128].rearrange("p (e r) -> p e r", r=R_),
                    self.pg2[:, 0:128].rearrange("p (e r) -> p e r", r=R_),
                    wv.broadcast_to([128, 8, R_]), op=ALU.mult))

            self.V(vh)
            dv_g = dv.n

            def pt_(t, dv_g=dv_g, prev=gt_prev, first=(tt == 0)):
                if first:
                    t.wait_ge(dv.h, self.d0_done_dv)
                t.wait_ge(dv.h, dv_g)
                if prev is not None:
                    t.wait_ge(dv.h, prev)
                pe.inc(t.transpose(self.psA[:, 0:128], gbuf[:, 1, 0:128],
                                   self.ident[:]))

            self.P(pt_)
            pe_gt = pe.n

            def vgt(v, tt=tt, pe_gt=pe_gt):
                v.wait_ge(pe.h, pe_gt)
                dv.inc(v.tensor_copy(big0[:, OF_GT + tt * 128:OF_GT + (tt + 1) * 128],
                                     self.psA[:, 0:128]))

            self.V(vgt)
            gt_prev = dv.n
        gates_dv = self.dv.n

        gsems = (g0, g1, self.g2, self.g3, self.g4)
        nslot = 6
        ssem = lambda hg: gsems[(hg % nslot) % 5]
        half_cons = {}
        half_ld = {}
        bank_evac = {}
        xslot_cons = {}
        st_state = {}
        it = 0
        for ob in range(8):
            for qf in range(4):
                hg = ob * 4 + qf
                gc = ssem(hg)

                def gw(g, ob=ob, qf=qf, hg=hg, gc=gc,
                       cons=half_cons.get(hg - nslot)):
                    if cons is not None:
                        g.wait_ge(pe.h, cons)
                    if hg == 0:
                        g.wait_ge(dv.h, ln2_dv)
                    self._dma(gc, g).inc(g.dma_start(
                        out=wtslab[:, hg % nslot, :, :].rearrange("p a b -> p (a b)"),
                        in_=self.wt_dev[ob, qf]), 16)

                self.G(gw)
                half_ld[hg] = gc.n

            for tt in range(8):
                par = it % 2
                bank = (self.pvA, self.pvB)[par]
                xsl = (x0, x1)[par]

                def yx(y, ob=ob, tt=tt, par=par, xsl=xsl,
                       cons=xslot_cons.get(par), first=(it < 2)):
                    if first:
                        y.wait_ge(self.so.h, x2_so)
                        y.wait_ge(dv.h, ln2_dv)
                    if cons is not None:
                        y.wait_ge(dv.h, cons)
                    self._dma(xsl, y).inc(y.dma_start(
                        out=xst[:, par, :],
                        in_=self.x2_dram[tt * 128:(tt + 1) * 128,
                                         ob * 512:(ob + 1) * 512]), 16)

                self.Y(yx)
                x_ld = xsl.n

                def vxb(v, ob=ob, par=par, x_ld=x_ld, xsl=xsl, gi_bb=gi_bb):
                    v.wait_ge(xsl.h, x_ld)
                    v.wait_ge(self.gi.h, gi_bb)
                    dv.inc(v.tensor_tensor(xst[:, par, :], xst[:, par, :],
                                           bb[:, ob * 512:(ob + 1) * 512],
                                           op=ALU.add))

                self.V(vxb)
                dv_xb = dv.n

                def p(t, ob=ob, tt=tt, bank=bank,
                      ws=tuple((ob * 4 + q, half_ld[ob * 4 + q]) for q in range(4)),
                      first=(it == 0), ev=bank_evac.get(par)):
                    if first:
                        t.wait_ge(dv.h, gates_dv)
                    for hg, ld in ws:
                        t.wait_ge(ssem(hg).h, ld)
                    if ev is not None:
                        t.wait_ge(dv.h, ev)
                    for ic in range(NIC):
                        t.matmul(bank[:],
                                 xh2[:, ic, tt * 128:(tt + 1) * 128],
                                 wtslab[:, (ob * 4 + ic // 8) % nslot, ic % 8, :],
                                 start=(ic == 0), stop=False)
                    pe.inc(t.matmul(bank[:],
                                    big0[:, OF_GT + tt * 128:OF_GT + (tt + 1) * 128],
                                    big0[:, OF_BM + ob * 512:OF_BM + (ob + 1) * 512],
                                    start=False, stop=True))

                self.P(p)
                pe_mm = pe.n
                for q in range(4):
                    half_cons[ob * 4 + q] = pe.n

                def vev(v, par=par, bank=bank, pe_mm=pe_mm, dv_xb=dv_xb,
                        stprev=st_state.get(par)):
                    v.wait_ge(pe.h, pe_mm)
                    v.wait_ge(dv.h, dv_xb)
                    if stprev is not None:
                        v.wait_ge(self.so.h, stprev)
                    dv.inc(v.tensor_tensor(x3st[:, par, :], bank[:], xst[:, par, :],
                                           op=ALU.add))

                self.V(vev)
                bank_evac[par] = dv.n
                xslot_cons[par] = dv.n
                dv_x3 = dv.n

                def yo(y, ob=ob, tt=tt, par=par, dv_x3=dv_x3):
                    y.wait_ge(dv.h, dv_x3)
                    self._dma(self.so, y).inc(y.dma_start(
                        out=self.out[tt * 128:(tt + 1) * 128,
                                     ob * 512:(ob + 1) * 512],
                        in_=x3st[:, par, :]), 16)

                self.Y(yo)
                st_state[par] = self.so.n
                it += 1

    def emit_tail(self):
        total = self.so.n

        def y(y):
            y.wait_ge(self.so.h, total)

        self.Y(y)


_NC_CACHE = {}


def _get_nc(dump=()):
    key = tuple(sorted(dump))
    if key not in _NC_CACHE:
        _NC_CACHE[key] = KB(dump).build()
    return _NC_CACHE[key]


def _tables():
    inv = 1.0 / (500000.0 ** (np.arange(0, 128, 2, dtype=np.float64) / 128.0))
    wavelen = 2.0 * np.pi / inv
    low_wl, high_wl = 8192.0 / 1.0, 8192.0 / 4.0
    inv_scaled = np.where(wavelen > low_wl, inv / 8.0, inv)
    smooth = (8192.0 / wavelen - 1.0) / (4.0 - 1.0)
    smoothed = (1.0 - smooth) / 8.0 * inv + smooth * inv
    is_medium = (wavelen <= low_wl) & (wavelen >= high_wl)
    invf = np.where(is_medium, smoothed, inv_scaled)
    pos = np.arange(T, dtype=np.float64)
    ang = invf[:, None] * pos[None, :]
    cos = np.cos(ang)
    sin = np.sin(ang)
    cosT = np.concatenate([cos, cos], 0).astype(np.float32)
    sinS = np.concatenate([-sin, sin], 0).astype(np.float32)
    p = np.arange(128)[:, None]
    f = np.arange(512)[None, :]
    blocks = [np.where(cc * 128 + p <= f, 0.0, NEG).astype(np.float32)
              for cc in range(4)]
    masksT = np.concatenate(blocks, axis=1)
    ident = np.eye(128, dtype=np.float32)
    return cosT, sinS, masksT, ident


def kernel(L0_attn_x, L0_attn_Wq, L0_attn_Wk, L0_attn_Wv, L0_attn_Wo,
           L0_ffn_W, L0_ffn_b, L0_ffn_Wq, L0_ffn_Wk, L0_ffn_A, L0_ffn_Bm,
           _dump=(), _nc=None):
    A = lambda a: np.ascontiguousarray(np.asarray(a, dtype=np.float32))
    x = A(L0_attn_x)
    wq, wk, wv, wo = A(L0_attn_Wq), A(L0_attn_Wk), A(L0_attn_Wv), A(L0_attn_Wo)
    w, bf = A(L0_ffn_W), A(L0_ffn_b)
    wqg, wkg = A(L0_ffn_Wq), A(L0_ffn_Wk)
    a_, bm = A(L0_ffn_A), A(L0_ffn_Bm)

    cosT, sinS, masksT, ident = _tables()
    wT = A(w.T.reshape(4, 8, 128, 8, 512).transpose(3, 0, 2, 1, 4)
           .reshape(8, 4, 128, 8 * 512))
    wqgT = A(wqg.T.reshape(NIC, 128, 256).transpose(1, 0, 2).reshape(128, NIC * 256))
    wkgT = A(wkg.T.reshape(NIC, 128, 256).transpose(1, 0, 2).reshape(128, NIC * 256))
    aT = A(a_.transpose(2, 0, 1).reshape(NIC, 128, 128)
           .transpose(1, 0, 2).reshape(128, NIC * 128))
    bmT = A(bm.transpose(0, 2, 1).reshape(E_ * R_, D))
    brow = A(bf.reshape(1, D))

    nc = _nc if _nc is not None else _get_nc(_dump)
    in_maps = []
    for core in range(8):
        b, r = core // 4, core % 4
        in_maps.append({
            "x": A(x[b, TOK * r:TOK * (r + 1), :]),
            "wqT": A(wq.T[:, 1024 * r:1024 * (r + 1)]
                     .reshape(NIC, 128, QH, 128).transpose(2, 1, 0, 3)
                     .reshape(QH, 128, NIC * 128)),
            "wkT": A(wk.T[:, 256 * r:256 * (r + 1)]
                     .reshape(NIC, 128, KVH, 128).transpose(2, 1, 0, 3)
                     .reshape(KVH, 128, NIC * 128)),
            "wvT": A(wv.T[:, 256 * r:256 * (r + 1)]
                     .reshape(NIC, 128, 256).transpose(1, 0, 2)
                     .reshape(128, NIC * 256)),
            "woT": A(wo.T[1024 * r:1024 * (r + 1), :]
                     .reshape(8, 128, 8, 512).transpose(2, 1, 0, 3)
                     .reshape(8, 128, 8 * 512)),
            "wT": wT,
            "bffn": brow,
            "wqgT": wqgT,
            "wkgT": wkgT,
            "aT": aT,
            "bmT": bmT,
            "cosT": cosT,
            "sinS": sinS,
            "masksT": masksT,
            "ident": ident,
        })

    res = run_bass_kernel_spmd(nc, in_maps, core_ids=list(range(8)))
    out = np.empty((B, T, D), np.float32)
    for core in range(8):
        b, r = core // 4, core % 4
        out[b, TOK * r:TOK * (r + 1), :] = res.results[core]["out"]
    if _dump:
        return out, res
    return out


# version 13
# speedup vs baseline: 1.0308x; 1.0308x over previous
import math
import numpy as np
import ml_dtypes
from contextlib import ExitStack

import concourse.bass as bass
import concourse.mybir as mybir
from concourse.bass_utils import run_bass_kernel_spmd

F32 = mybir.dt.float32
BF16 = mybir.dt.bfloat16
AF = mybir.ActivationFunctionType
ALU = mybir.AluOpType
AX = mybir.AxisListType

B, T, D = 2, 4096, 4096
NH, NKV, HD = 32, 8, 128
E_, DK, R_ = 8, 32, 16
TOK = 1024
NIC = D // 128
QH, KVH = 8, 2
SCALE = 1.0 / math.sqrt(HD)
SHIFT = 10.0
GSC = 1.0 / math.sqrt(DK)
EPS = 1e-5
NEG = -1.0e9

OF_WQG, OF_WKG, OF_AT, OF_BM, OF_GT = 0, 8192, 16384, 20480, 24576


class Ctr:
    def __init__(self, stack, nc, name):
        self.h = stack.enter_context(nc.semaphore(name))
        self.n = 0

    def inc(self, ins, k=1):
        ins.then_inc(self.h, k)
        self.n += k
        return self.n


class KB:
    def __init__(self, dump=(), stop="FFN"):
        self.dump = set(dump)
        self.stop = stop
        self.nc = bass.Bass("TRN2", target_bir_lowering=False)

    def _dma(self, ctr, eng):
        eng.wait_ge(ctr.h, ctr.n)
        return ctr

    def P(self, f): self.blk.tensor(f)
    def V(self, f): self.blk.vector(f)
    def S(self, f): self.blk.scalar(f)
    def G(self, f): self.blk.gpsimd(f)
    def Y(self, f): self.blk.sync(f)

    def build(self):
        nc = self.nc
        st = self.stack = ExitStack()
        st.__enter__()

        di = lambda n, shp, dt=F32: nc.dram_tensor(n, shp, dt, kind="ExternalInput").ap()
        self.x_in = di("x", [TOK, D])
        self.wq_dev = di("wqT", [QH, 128, NIC * 128])
        self.wk_dev = di("wkT", [KVH, 128, NIC * 128])
        self.wv_dev = di("wvT", [128, NIC * 256])
        self.wo_dev = di("woT", [8, 128, 8 * 512])
        self.wt_dev = di("wT", [8, 4, 128, 8 * 512])
        self.b_in = di("bffn", [1, D])
        self.wqg_dev = di("wqgT", [128, NIC * 256])
        self.wkg_dev = di("wkgT", [128, NIC * 256])
        self.at_dev = di("aT", [128, NIC * 128])
        self.bmT_in = di("bmT", [E_ * R_, D])
        self.cosT = di("cosT", [128, T])
        self.sinS = di("sinS", [128, T])
        self.masksT_in = di("masksT", [128, 4 * 512])
        self.ident_in = di("ident", [128, 128])

        self.out = nc.dram_tensor("out", [TOK, D], F32, kind="ExternalOutput").ap()

        dint = lambda n, shp, dt: nc.dram_tensor(n, shp, dt, kind="Internal").ap()
        self.xh_cc = [dint("xh_cc0", [D, TOK // 2], BF16),
                      dint("xh_cc1", [D, TOK // 2], BF16)]
        self.xh_all = [dint("xh_all0", [4 * D, TOK // 2], BF16),
                       dint("xh_all1", [4 * D, TOK // 2], BF16)]
        self.pw_dram = [dint(f"pw_dram{j}", [T, D // 4], BF16) for j in range(4)]
        self.x2a_dram = [dint(f"x2a_dram{j}", [TOK, D // 4], BF16) for j in range(4)]
        self.x2_dram = dint("x2_dram", [TOK, D], F32)

        self.dumps = {}
        for name, shp, dt in (
            ("d_xh", [4 * D, TOK], BF16),
            ("d_q", [128, QH * T], BF16),
            ("d_k", [128, KVH * T], BF16),
            ("d_v", [128, KVH * T], BF16),
            ("d_ao", [128, QH * T], BF16),
            ("d_x2", [TOK, D], F32),
            ("d_x2a", [TOK, D], BF16),
            ("d_st", [128, 64], F32),
        ):
            if name in self.dump:
                self.dumps[name] = nc.dram_tensor(name, shp, dt, kind="ExternalOutput").ap()

        sb = lambda n, shp, dt: st.enter_context(nc.sbuf_tensor(n, shp, dt))
        self.big0 = sb("big0", [128, 32768], BF16)
        self.big1 = sb("big1", [128, 32768], BF16)
        self.stg = sb("stg", [128, 2, 512], F32)
        self.ident = sb("ident_sb", [128, 128], F32)
        self.ident_bf = sb("ident_bf", [128, 128], BF16)
        self.ones_c = sb("ones_c", [128, 1], BF16)
        self.ones_r = sb("ones_r", [1, 128], F32)
        self.stat = sb("stat", [128, 16], F32)
        self.lrow = sb("lrow", [1, 512], F32)
        self.csts = sb("csts", [128, 2], F32)
        nc.const_aps.register(F32, EPS, self.csts[:, 0:1])
        nc.const_aps.register(F32, -SHIFT, self.csts[:, 1:2])

        ps = lambda n, shp: st.enter_context(nc.psum_tensor(n, shp, F32))
        self.sT2 = ps("sT2", [128, 1024])
        self.psA = self.sT2[:, 0:512]
        self.psB = self.sT2[:, 512:1024]
        self.pvA = ps("pvA", [128, 512])
        self.pvB = ps("pvB", [128, 512])
        self.psl = ps("psl", [128, 512])
        self.pg0 = ps("pg0", [128, 512])
        self.pg1 = ps("pg1", [128, 512])
        self.pg2 = ps("pg2", [128, 512])

        C = lambda n: Ctr(st, nc, n)
        self.pe, self.dv, self.ac = C("pe"), C("dv"), C("ac")
        self.g0, self.g1, self.g2, self.gi = C("g0"), C("g1"), C("g2"), C("gi")
        self.g3, self.g4 = C("g3"), C("g4")
        self.x0, self.x1, self.xq, self.xi = C("x0"), C("x1"), C("xq"), C("xi")
        self.so = C("so")
        self.cc1, self.cc2 = C("cc1"), C("cc2")

        self.blk = st.enter_context(nc.Block())

        self.emit_init()
        with nc.sbuf_tensor("xpair", [128, 3, 4096], F32) as xpair, \
             nc.sbuf_tensor("sqsink", [128, 512], F32) as sink:
            self.emit_ln_transpose(xpair, None, sink, src="x", dst_big=self.big0)
        self.emit_gather1()
        if self.stop == "A":
            self.emit_tail()
            st.__exit__(None, None, None)
            return self.nc
        with nc.sbuf_tensor("kT", [128, KVH * T], BF16) as kT_sb, \
             nc.sbuf_tensor("vsb", [128, KVH * T], BF16) as v_sb:
            with nc.sbuf_tensor("wslab", [128, 2, 4096], BF16) as wslab, \
                 nc.sbuf_tensor("wvsb", [128, NIC * 256], BF16) as wv_sb, \
                 nc.sbuf_tensor("cs", [128, 2, 512], F32) as cos_sb, \
                 nc.sbuf_tensor("sn", [128, 2, 512], F32) as sin_sb:
                self.emit_qkv(kT_sb, v_sb, wslab, wv_sb, cos_sb, sin_sb)
            if self.stop != "QKV":
                with nc.sbuf_tensor("masks", [128, 4 * 512], BF16) as mask_sb, \
                     nc.sbuf_tensor("ptc", [128, 6, 512], BF16) as ptc, \
                     nc.sbuf_tensor("rbst", [128, 512], F32) as rbst:
                    self.emit_attention(kT_sb, v_sb, mask_sb, ptc, rbst)
        if self.stop not in ("QKV", "ATT"):
            with nc.sbuf_tensor("woslab", [128, 2, 8, 512], BF16) as woslab, \
                 nc.sbuf_tensor("pwst", [128, 2, 512], BF16) as pwst:
                self.emit_wo_partial(woslab, pwst)
            self.emit_scatter2()
            with nc.sbuf_tensor("x2t", [128, 3, 4096], F32) as x2t, \
                 nc.sbuf_tensor("x2a", [128, 4096], BF16) as x2a, \
                 nc.sbuf_tensor("sqsink2", [128, 512], F32) as sink2:
                self.emit_ln_transpose(x2t, x2a, sink2, src="x2", dst_big=self.big1)
            if self.stop != "LN2":
                with nc.sbuf_tensor("wtslab", [128, 6, 8, 512], BF16) as wtslab, \
                     nc.sbuf_tensor("bb", [128, 4096], BF16) as bb, \
                     nc.sbuf_tensor("gbuf", [128, 5, 256], F32) as gbuf, \
                     nc.sbuf_tensor("x3st", [128, 2, 512], F32) as x3st:
                    self.emit_ffn(wtslab, bb, gbuf, self.stg, x3st)
        self.emit_tail()

        st.__exit__(None, None, None)
        return self.nc

    def emit_init(self):
        def g(g):
            self._dma(self.gi, g).inc(g.dma_start(out=self.ident[:], in_=self.ident_in), 16)
            self._dma(self.gi, g).inc(g.dma_start(out=self.ident_bf[:], in_=self.ident_in), 16)

        self.G(g)

        def v(v):
            v.memset(self.ones_c[:], 1.0)
            v.memset(self.csts[:, 0:1], EPS)
            v.memset(self.csts[:, 1:2], -SHIFT)
            self.dv.inc(v.memset(self.ones_r[:], 1.0))

        self.V(v)
        self.init_gi = self.gi.n

    def emit_ln_transpose(self, xpair, x2a, sink, src, dst_big):
        pe, dv, ac = self.pe, self.dv, self.ac
        x0, x1 = self.x0, self.x1
        bank_evac = {}
        slot_free = {}
        x2a_free = None

        for tt in range(8):
            slot = tt % 3
            xc = (x0, x1)[tt % 2]
            xtile = xpair[:, slot, :]

            def y(y, tt=tt, xc=xc, xtile=xtile, cons=slot_free.get(slot),
                  x2a_free=x2a_free):
                if src == "x2" and tt < 3:
                    y.wait_ge(self.cc2.h, 4)
                if cons is not None:
                    y.wait_ge(pe.h, cons)
                self._dma(xc, y).inc(y.dma_start(out=xtile,
                                   in_=self.x_in[tt * 128:(tt + 1) * 128, :]), 16)
                if src == "x2":
                    if x2a_free is not None:
                        y.wait_ge(dv.h, x2a_free)
                    for j in range(4):
                        self._dma(xc, y).inc(y.dma_start(
                            out=x2a[:, j * 1024:(j + 1) * 1024],
                            in_=self.x2a_dram[j][tt * 128:(tt + 1) * 128, :]), 16)

            self.Y(y)
            ldv = xc.n

            y2_so = None
            if src == "x2":
                def va(v, xtile=xtile, ldv=ldv, xc=xc):
                    v.wait_ge(xc.h, ldv)
                    dv.inc(v.tensor_tensor(xtile, xtile, x2a[:], op=ALU.add))

                self.V(va)
                dv_x2 = dv.n
                x2a_free = dv.n

                def y2(y, tt=tt, xtile=xtile, dv_x2=dv_x2):
                    y.wait_ge(dv.h, dv_x2)
                    self._dma(self.so, y).inc(y.dma_start(
                        out=self.x2_dram[tt * 128:(tt + 1) * 128, :], in_=xtile), 16)

                self.Y(y2)
                y2_so = self.so.n
                ready = ("dv", dv_x2)
            else:
                ready = ("xc", ldv)

            def v1(v, xtile=xtile, ready=ready, xc=xc):
                if ready[0] == "xc":
                    v.wait_ge(xc.h, ready[1])
                else:
                    v.wait_ge(dv.h, ready[1])
                dv.inc(v.reduce_sum(self.stat[:, 0:1], xtile, axis=AX.X))

            self.V(v1)
            dv_s1 = dv.n
            prev_var = getattr(self, "_lnt_prev_xhat", None)

            def s1(s, xtile=xtile, ready=ready, xc=xc, prev_var=prev_var):
                if ready[0] == "xc":
                    s.wait_ge(xc.h, ready[1])
                else:
                    s.wait_ge(dv.h, ready[1])
                if prev_var is not None:
                    s.wait_ge(dv.h, prev_var)
                for c in range(8):
                    ac.inc(s.activation(sink[:], xtile[:, c * 512:(c + 1) * 512],
                                        AF.Square, bias=0.0, scale=1.0,
                                        accum_out=self.stat[:, 8 + c:9 + c]))
                    if c < 7:
                        s.wait_ge(ac.h, ac.n)

            self.S(s1)
            ac_sq = ac.n

            def v2(v, ac_sq=ac_sq, dv_s1=dv_s1):
                v.wait_ge(ac.h, ac_sq)
                v.wait_ge(dv.h, dv_s1)
                v.reduce_sum(self.stat[:, 1:2], self.stat[:, 8:16], axis=AX.X)
                dv.inc(v.tensor_scalar_mul(self.stat[:, 2:3], self.stat[:, 0:1],
                                           -1.0 / D))
                v.wait_ge(dv.h, dv.n)
                dv.inc(v.tensor_mul(self.stat[:, 3:4], self.stat[:, 2:3],
                                    self.stat[:, 2:3]))
                v.wait_ge(dv.h, dv.n)
                dv.inc(v.tensor_scalar(self.stat[:, 4:5], self.stat[:, 1:2],
                                       1.0 / D, self.stat[:, 3:4],
                                       op0=ALU.mult, op1=ALU.subtract))

            self.V(v2)
            dv_var = dv.n

            def s2(s, dv_var=dv_var):
                s.wait_ge(dv.h, dv_var)
                ac.inc(s.activation(self.stat[:, 5:6], self.stat[:, 4:5], AF.Sqrt,
                                    bias=EPS, scale=1.0))

            self.S(s2)
            ac_std = ac.n

            def v3(v, ac_std=ac_std, xtile=xtile, y2_so=y2_so):
                v.wait_ge(ac.h, ac_std)
                if y2_so is not None:
                    v.wait_ge(self.so.h, y2_so)
                dv.inc(v.reciprocal(self.stat[:, 6:7], self.stat[:, 5:6]))
                v.wait_ge(dv.h, dv.n)
                dv.inc(v.tensor_scalar(xtile, xtile, self.stat[:, 2:3],
                                       self.stat[:, 6:7], op0=ALU.add, op1=ALU.mult))

            self.V(v3)
            dv_xhat = dv.n
            self._lnt_prev_xhat = dv_xhat

            for grp in range(8):
                par = grp % 2
                bank = (self.pg0, self.pg1)[par]

                def p(t, grp=grp, bank=bank, dv_xhat=dv_xhat, xtile=xtile,
                      prev=bank_evac.get(par), first=(tt == 0 and grp == 0)):
                    if first:
                        t.wait_ge(self.gi.h, self.init_gi)
                    t.wait_ge(dv.h, dv_xhat)
                    if prev is not None:
                        t.wait_ge(dv.h, prev)
                    for j in range(4):
                        ic = grp * 4 + j
                        ins = t.transpose(bank[:, j * 128:(j + 1) * 128],
                                          xtile[:, ic * 128:(ic + 1) * 128],
                                          self.ident[:])
                    pe.inc(ins)

                self.P(p)
                pe_tr = pe.n

                def v4(v, grp=grp, bank=bank, tt=tt, pe_tr=pe_tr):
                    v.wait_ge(pe.h, pe_tr)
                    o = dst_big[:].rearrange("p (ic t) -> p ic t", t=1024)
                    dv.inc(v.tensor_copy(
                        o[:, grp * 4:(grp + 1) * 4, tt * 128:(tt + 1) * 128],
                        bank[:].rearrange("p (j t) -> p j t", t=128)))

                self.V(v4)
                bank_evac[par] = dv.n
            slot_free[slot] = pe.n
            if src == "x" and tt == 3:
                self.ln1_half0_dv = self.dv.n
        if src == "x":
            self.ln1_done_dv = self.dv.n
        else:
            self.ln2_done_dv = self.dv.n
            self.x2_stores_so = self.so.n
            if "d_x2" in self.dumps:
                need = self.so.n

                def yd(y):
                    y.wait_ge(self.so.h, need)
                    self._dma(self.so, y).inc(y.dma_start(out=self.dumps["d_x2"],
                                            in_=self.x2_dram), 16)

                self.Y(yd)

    def emit_gather1(self):
        so = self.so
        needs = (self.ln1_half0_dv, self.ln1_done_dv)
        big = self.big0[:].rearrange("p (ic t) -> p ic t", t=1024)
        for hh in range(2):
            def y(y, hh=hh):
                y.wait_ge(self.dv.h, needs[hh])
                dst = self.xh_cc[hh].rearrange("(ic p) t -> p ic t", p=128)
                for h in range(4):
                    self._dma(so, y).inc(y.dma_start(
                        out=dst[:, h * 8:(h + 1) * 8, :],
                        in_=big[:, h * 8:(h + 1) * 8, hh * 512:(hh + 1) * 512]), 16)

            self.Y(y)
            need_so = so.n

            def g(g, hh=hh, need_so=need_so):
                g.wait_ge(so.h, need_so)
                self.cc1.inc(g.collective_compute(
                    "AllGather", ALU.bypass,
                    replica_groups=[[0, 1, 2, 3], [4, 5, 6, 7]],
                    ins=[self.xh_cc[hh]], outs=[self.xh_all[hh]]))

            self.G(g)

        if "d_xh" in self.dumps:
            def yd(y):
                y.wait_ge(self.cc1.h, 2)
                for hh in range(2):
                    self._dma(so, y).inc(y.dma_start(
                        out=self.dumps["d_xh"].rearrange(
                            "r (h t) -> r h t", h=2)[:, hh, :],
                        in_=self.xh_all[hh]), 16)
            self.Y(yd)

    def emit_qkv(self, kT_sb, v_sb, wslab, wv_sb, cos_sb, sin_sb):
        pe, dv, ac = self.pe, self.dv, self.ac
        g0, g1, xq, xi = self.g0, self.g1, self.xq, self.xi

        def gwv(g):
            g.wait_ge(dv.h, self.ln1_done_dv)
            self._dma(self.gi, g).inc(g.dma_start(out=wv_sb[:], in_=self.wv_dev), 16)

        self.G(gwv)
        gi_wv = self.gi.n

        qk_banks = (self.psA, self.psB, self.pg0[:], self.pg1[:])
        bank_evac = {}
        slab_cons = {}
        self._qkv_sli = 0
        self._qkv_bc = 0
        xq_cons = None
        rope_done = None

        for qr in range(4):
            xq_ld_h = [None, None]
            for hh in range(2):
                def yq(y, qr=qr, hh=hh, xq_cons=xq_cons):
                    if qr == 0:
                        y.wait_ge(self.cc1.h, hh + 1)
                    elif hh == 0:
                        y.wait_ge(pe.h, xq_cons)
                    dst = self.big0[:].rearrange("p (ic t) -> p ic t", t=1024)
                    src = self.xh_all[hh][qr * D:(qr + 1) * D, :].rearrange(
                        "(ic p) t -> p ic t", p=128)
                    for h in range(2):
                        self._dma(xq, y).inc(y.dma_start(
                            out=dst[:, h * 16:(h + 1) * 16,
                                    hh * 512:(hh + 1) * 512],
                            in_=src[:, h * 16:(h + 1) * 16, :]), 16)

                self.Y(yq)
                xq_ld_h[hh] = xq.n
            xq_ld = xq.n
            big = self.big0[:].rearrange("p (ic t) -> p ic t", t=1024)

            def ycs(y, qr=qr, rope_done=rope_done):
                y.wait_ge(dv.h, self.ln1_done_dv if rope_done is None else rope_done)
                for tb in range(2):
                    c0 = qr * 1024 + tb * 512
                    self._dma(xi, y).inc(y.dma_start(out=cos_sb[:, tb, :],
                                       in_=self.cosT[:, c0:c0 + 512]), 16)
                    self._dma(xi, y).inc(y.dma_start(out=sin_sb[:, tb, :],
                                       in_=self.sinS[:, c0:c0 + 512]), 16)

            self.Y(ycs)
            xi_cs = xi.n

            work = ([(oc, (0, 1)) for oc in range(QH + KVH)] if qr > 0 else
                    [(oc, (0,)) for oc in range(QH + KVH)] +
                    [(oc, (1,)) for oc in range(QH + KVH)])
            for oc, tbs in work:
                is_q = oc < QH
                wsrc = self.wq_dev[oc] if is_q else self.wk_dev[oc - QH]
                slot = self._qkv_sli % 2
                self._qkv_sli += 1
                gc = (g0, g1)[slot]

                def gw(g, wsrc=wsrc, slot=slot, gc=gc,
                       cons=slab_cons.get(slot)):
                    if cons is not None:
                        g.wait_ge(pe.h, cons)
                    else:
                        g.wait_ge(dv.h, self.ln1_done_dv)
                    self._dma(gc, g).inc(g.dma_start(out=wslab[:, slot, :], in_=wsrc), 16)

                self.G(gw)
                gw_ld = gc.n

                for tb in tbs:
                    bidx = self._qkv_bc % 4
                    self._qkv_bc += 1
                    bank = qk_banks[bidx]

                    def p(t, tb=tb, bank=bank, slot=slot, gw_ld=gw_ld, gc=gc,
                          xq_ld=xq_ld_h[tb], ev=bank_evac.get(bidx)):
                        t.wait_ge(gc.h, gw_ld)
                        t.wait_ge(xq.h, xq_ld)
                        if ev is not None:
                            t.wait_ge(dv.h, ev)
                        for ic in range(NIC):
                            ins = t.matmul(bank,
                                           wslab[:, slot, ic * 128:(ic + 1) * 128],
                                           big[:, ic, tb * 512:(tb + 1) * 512],
                                           start=(ic == 0), stop=(ic == NIC - 1))
                        pe.inc(ins)

                    self.P(p)
                    pe_mm = pe.n
                    slab_cons[slot] = pe.n

                    def vr(v, oc=oc, tb=tb, qr=qr, bank=bank, pe_mm=pe_mm,
                           is_q=is_q, xi_cs=xi_cs, dvfence=dv.n):
                        v.wait_ge(pe.h, pe_mm)
                        v.wait_ge(xi.h, xi_cs)
                        v.wait_ge(dv.h, dvfence)
                        tmp = self.stg[:, 0, :]
                        t2 = self.stg[:, 1, :]
                        sn = sin_sb[:, tb, :]
                        cs = cos_sb[:, tb, :]
                        v.tensor_tensor(tmp[0:64, :], bank[64:128, :], sn[0:64, :],
                                        op=ALU.mult)
                        v.tensor_tensor(tmp[64:128, :], bank[0:64, :], sn[64:128, :],
                                        op=ALU.mult)
                        dv.inc(v.tensor_tensor(t2[:], bank, cs, op=ALU.mult))
                        v.wait_ge(dv.h, dv.n)
                        s0 = (oc * T if is_q else (oc - QH) * T) + qr * 1024 + tb * 512
                        dst = (self.big1 if is_q else kT_sb)[:, s0:s0 + 512]
                        dv.inc(v.tensor_tensor(dst, t2[:], tmp[:], op=ALU.add))

                    self.V(vr)
                    bank_evac[bidx] = dv.n

            rope_done = dv.n

            for tt in range(8):
                cg = qr * 8 + tt

                def pv(t, tt=tt, gi_wv=gi_wv, xq_ld=xq_ld_h[tt // 4],
                       ev=bank_evac.get("v")):
                    t.wait_ge(self.gi.h, gi_wv)
                    t.wait_ge(xq.h, xq_ld)
                    if ev is not None:
                        t.wait_ge(dv.h, ev)
                    for ic in range(NIC):
                        ins = t.matmul(self.pg2[:, 0:256],
                                       big[:, ic, tt * 128:(tt + 1) * 128],
                                       wv_sb[:, ic * 256:(ic + 1) * 256],
                                       start=(ic == 0), stop=(ic == NIC - 1))
                    pe.inc(ins)

                self.P(pv)
                pe_v = pe.n
                xq_cons = pe.n

                def vv(v, cg=cg, pe_v=pe_v):
                    v.wait_ge(pe.h, pe_v)
                    dst = v_sb[:].rearrange("p (kv c d) -> p kv c d", kv=KVH, d=128)
                    src = self.pg2[:, 0:256].rearrange("p (kv d) -> p kv d", d=128)
                    dv.inc(v.tensor_copy(dst[:, :, cg, :], src))

                self.V(vv)
                bank_evac["v"] = dv.n

        self.qkv_done_dv = self.dv.n

        for nm, buf in (("d_q", self.big1[:, 0:QH * T]), ("d_k", kT_sb[:]),
                        ("d_v", v_sb[:])):
            if nm in self.dumps:
                def yd(y, nm=nm, buf=buf):
                    y.wait_ge(dv.h, self.qkv_done_dv)
                    self._dma(self.so, y).inc(y.dma_start(out=self.dumps[nm], in_=buf), 16)
                self.Y(yd)

    def emit_attention(self, kT_sb, v_sb, mask_sb, ptc, rbst):
        pe, dv, ac = self.pe, self.dv, self.ac

        def gm(g):
            g.wait_ge(self.dv.h, self.qkv_done_dv)
            self._dma(self.gi, g).inc(g.dma_start(out=mask_sb[:], in_=self.masksT_in), 16)

        self.G(gm)
        gi_m = self.gi.n
        qkv_dv = self.qkv_done_dv

        sT_banks = (self.psA, self.psB)
        pv_banks = (self.pvA, self.pvB)
        l_banks = (self.psl, self.pg2)
        LROW = (0, 32, 64, 0)

        chunk_list = []
        groups = []
        for kvh in range(KVH):
            for qh4 in range(4):
                qh = kvh * 4 + qh4
                for qc in range(8):
                    g = len(groups)
                    n = 4 * (qc + 1)
                    for ci in range(n):
                        bk, cc = ci // 4, ci % 4
                        chunk_list.append((g, kvh, qh, qc, bk * 4 + cc,
                                           cc if bk == qc else None,
                                           ci == 0, ci == n - 1))
                    groups.append((kvh, qh, qc))
        NCH = len(chunk_list)
        NG = len(groups)

        ac0 = ac.n
        exp_ac = lambda k: ac0 + k + 1
        dv0 = dv.n
        vf_dv = lambda g: dv0 + 6 * g + 3
        rbcopy_dv = lambda g: dv0 + 6 * g + 4
        aomult_dv = lambda g: dv0 + 6 * g + 6

        pe_sT = [None] * NCH
        pe_slotfree = [None] * NCH
        pe_rb = [None] * NG
        pe_pv = [None] * NG
        first_k = {}
        for k, (g, *_rest) in enumerate(chunk_list):
            first_k.setdefault(g, k)

        def P(t):
            pending_rb = []

            def emit_sT(k):
                g, kvh, qh, qc, kc, diag_cc, first, last = chunk_list[k]
                if k == 0:
                    t.wait_ge(dv.h, qkv_dv)
                    t.wait_ge(self.gi.h, gi_m)
                if k >= 2:
                    t.wait_ge(ac.h, exp_ac(k - 2))
                if first and g >= 2:
                    t.wait_ge(dv.h, aomult_dv(g - 2))
                ins = t.matmul(
                    sT_banks[k % 2],
                    kT_sb[:, kvh * T + kc * 128:kvh * T + (kc + 1) * 128],
                    self.big1[:, qh * T + qc * 512:qh * T + qc * 512 + 512],
                    start=True, stop=(diag_cc is None))
                if diag_cc is not None:
                    ins = t.matmul(sT_banks[k % 2], self.ident_bf[:],
                                   mask_sb[:, diag_cc * 512:(diag_cc + 1) * 512],
                                   start=False, stop=True)
                pe.inc(ins)
                pe_sT[k] = pe.n

            def emit_p2(k):
                g, kvh, qh, qc, kc, diag_cc, first, last = chunk_list[k]
                ci = k - first_k[g]
                t.wait_ge(ac.h, exp_ac(k))
                pe.inc(t.matmul(
                    pv_banks[g % 2][:],
                    v_sb[:, kvh * T + kc * 128:kvh * T + (kc + 1) * 128],
                    ptc[:, k % 6, :], start=first, stop=last))
                pe_slotfree[k] = pe.n
                if ci % 4 == 3:
                    nb = qc + 1
                    b = ci // 4
                    lb = l_banks[g % 2]
                    for j in range(4):
                        kk = k - 3 + j
                        row = LROW[j]
                        is_start = (b == 0) and (j != 3)
                        is_stop = (b == nb - 1) and (j != 0)
                        pe.inc(t.matmul(
                            lb[row:row + 1, :], self.ones_c[:],
                            ptc[:, kk % 6, :],
                            start=is_start, stop=is_stop,
                            tile_position=(0, row)))
                        pe_slotfree[kk] = pe.n
                if last:
                    pe_pv[g] = pe.n
                    pending_rb.append((g, k + 4))

            def emit_rb(g):
                t.wait_ge(dv.h, vf_dv(g))
                if g >= 2:
                    t.wait_ge(dv.h, rbcopy_dv(g - 2))
                rb = (self.pg0, self.pg1)[g % 2]
                pe.inc(t.matmul(rb[:], self.ones_r[:], self.lrow[:],
                                start=True, stop=True))
                pe_rb[g] = pe.n

            for k in range(NCH):
                emit_sT(k)
                while pending_rb and pending_rb[0][1] <= k:
                    emit_rb(pending_rb.pop(0)[0])
                if k >= 2:
                    emit_p2(k - 2)
            emit_p2(NCH - 2)
            emit_p2(NCH - 1)
            while pending_rb:
                emit_rb(pending_rb.pop(0)[0])

        self.P(P)

        def V(v):
            for g, (kvh, qh, qc) in enumerate(groups):
                lb = l_banks[g % 2]
                v.wait_ge(pe.h, pe_pv[g])
                v.wait_ge(dv.h, dv.n)
                dv.inc(v.tensor_copy(rbst[0:1, :], lb[0:1, :]))
                v.wait_ge(dv.h, dv.n)
                dv.inc(v.tensor_tensor(rbst[0:1, :], rbst[0:1, :], lb[32:33, :],
                                       op=ALU.add))
                v.wait_ge(dv.h, dv.n)
                dv.inc(v.tensor_tensor(self.lrow[:], rbst[0:1, :], lb[64:65, :],
                                       op=ALU.add))
                assert dv.n == vf_dv(g)
                v.wait_ge(pe.h, pe_rb[g])
                v.wait_ge(dv.h, dv.n)
                rb = (self.pg0, self.pg1)[g % 2]
                dv.inc(v.tensor_copy(rbst[:], rb[:]))
                assert dv.n == rbcopy_dv(g)
                v.wait_ge(dv.h, dv.n)
                dv.inc(v.reciprocal(rbst[:], rbst[:]))
                v.wait_ge(dv.h, dv.n)
                dst = self.big0[:, qh * T + qc * 512:qh * T + qc * 512 + 512]
                dv.inc(v.tensor_tensor(dst, pv_banks[g % 2][:], rbst[:],
                                       op=ALU.mult))
                assert dv.n == aomult_dv(g)

        self.V(V)

        def S(s):
            for k in range(NCH):
                s.wait_ge(pe.h, pe_sT[k])
                if k >= 6:
                    s.wait_ge(pe.h, pe_slotfree[k - 6])
                ac.inc(s.activation(ptc[:, k % 6, :], sT_banks[k % 2],
                                    AF.Exp, bias=-SHIFT, scale=SCALE))
                assert ac.n == exp_ac(k)

        self.S(S)

        self.attn_done_dv = self.dv.n
        if "d_ao" in self.dumps:
            def yd(y):
                y.wait_ge(dv.h, self.attn_done_dv)
                self._dma(self.so, y).inc(y.dma_start(out=self.dumps["d_ao"],
                                        in_=self.big0[:, 0:QH * T]), 16)
            self.Y(yd)

    def emit_wo_partial(self, woslab, pwst):
        pe, dv = self.pe, self.dv
        g0, g1, x0, x1 = self.g0, self.g1, self.x0, self.x1
        attn_dv = self.attn_done_dv
        slab_cons = {}
        bank_evac = {}
        store_state = {}
        it = 0

        for fb in range(8):
            slot = fb % 2
            gc = (g0, g1)[slot]

            def gw(g, fb=fb, slot=slot, gc=gc, cons=slab_cons.get(slot)):
                if cons is not None:
                    g.wait_ge(pe.h, cons)
                else:
                    g.wait_ge(dv.h, attn_dv)
                self._dma(gc, g).inc(g.dma_start(
                    out=woslab[:, slot, :, :].rearrange("p a b -> p (a b)"),
                    in_=self.wo_dev[fb]), 16)

            self.G(gw)
            gw_ld = gc.n

            for tt in range(32):
                par = it % 2
                bank = (self.psA, self.psB)[par]

                def p(t, tt=tt, slot=slot, bank=bank, gw_ld=gw_ld, gc=gc,
                      first=(it == 0), ev=bank_evac.get(par)):
                    if first:
                        t.wait_ge(dv.h, attn_dv)
                    t.wait_ge(gc.h, gw_ld)
                    if ev is not None:
                        t.wait_ge(dv.h, ev)
                    for qh in range(QH):
                        ins = t.matmul(
                            bank[:],
                            self.big0[:, qh * T + tt * 128:qh * T + (tt + 1) * 128],
                            woslab[:, slot, qh, :],
                            start=(qh == 0), stop=(qh == QH - 1))
                    pe.inc(ins)

                self.P(p)
                pe_mm = pe.n
                slab_cons[slot] = pe.n

                def ve(v, par=par, bank=bank, pe_mm=pe_mm,
                       stfree=store_state.get(par)):
                    v.wait_ge(pe.h, pe_mm)
                    if stfree is not None:
                        v.wait_ge((x0, x1)[par].h, stfree)
                    dv.inc(v.tensor_copy(pwst[:, par, :], bank[:]))

                self.V(ve)
                bank_evac[par] = dv.n
                dv_e = dv.n

                def ys(y, fb=fb, tt=tt, par=par, dv_e=dv_e):
                    y.wait_ge(dv.h, dv_e)
                    self._dma((x0, x1)[par], y).inc(y.dma_start(
                        out=self.pw_dram[fb // 2][tt * 128:(tt + 1) * 128,
                                                  (fb % 2) * 512:(fb % 2) * 512 + 512],
                        in_=pwst[:, par, :]), 16)

                self.Y(ys)
                store_state[par] = (x0, x1)[par].n
                it += 1

            if fb % 2 == 1:
                j = fb // 2
                sx0, sx1 = self.x0.n, self.x1.n

                def g(g, j=j, sx0=sx0, sx1=sx1):
                    g.wait_ge(self.x0.h, sx0)
                    g.wait_ge(self.x1.h, sx1)
                    self.cc2.inc(g.collective_compute(
                        "ReduceScatter", ALU.add,
                        replica_groups=[[0, 1, 2, 3], [4, 5, 6, 7]],
                        ins=[self.pw_dram[j]], outs=[self.x2a_dram[j]]))

                self.G(g)

        self.wo_done_pe = self.pe.n
        self.d0_done_dv = self.dv.n

    def emit_scatter2(self):
        if "d_x2a" in self.dumps:
            def yd(y):
                y.wait_ge(self.cc2.h, 4)
                for j in range(4):
                    self._dma(self.so, y).inc(y.dma_start(
                        out=self.dumps["d_x2a"][:, j * 1024:(j + 1) * 1024],
                        in_=self.x2a_dram[j]), 16)
            self.Y(yd)

    def emit_ffn(self, wtslab, bb, gbuf, xst, x3st):
        pe, dv, ac = self.pe, self.dv, self.ac
        g0, g1, x0, x1, xi = self.g0, self.g1, self.x0, self.x1, self.xi
        ln2_dv = self.ln2_done_dv
        x2_so = self.x2_stores_so
        wo_pe = self.wo_done_pe
        big0, big1 = self.big0, self.big1

        def gco(g):
            g.wait_ge(pe.h, wo_pe)
            self._dma(self.gi, g).inc(g.dma_start(out=big0[:, OF_WQG:OF_WQG + 8192],
                                    in_=self.wqg_dev), 16)
            self._dma(self.gi, g).inc(g.dma_start(out=big0[:, OF_WKG:OF_WKG + 8192],
                                    in_=self.wkg_dev), 16)
            self._dma(self.gi, g).inc(g.dma_start(out=big0[:, OF_AT:OF_AT + 4096],
                                    in_=self.at_dev), 16)
            self._dma(self.gi, g).inc(g.dma_start(out=big0[:, OF_BM:OF_BM + 4096],
                                    in_=self.bmT_in), 16)

        self.G(gco)
        gi_c = self.gi.n

        def gbb(g):
            g.wait_ge(dv.h, ln2_dv)
            src = self.b_in.broadcast_to([1, D, 128]).transpose([2, 0, 1]) \
                .rearrange("p one d -> p (one d)")
            self._dma(self.gi, g).inc(g.dma_start(out=bb[:], in_=src), 16)

        self.G(gbb)
        gi_bb = self.gi.n

        xh2 = big1[:].rearrange("p (ic t) -> p ic t", t=1024)

        gate_prev = None
        gt_prev = None
        for tt in range(8):
            def pq(t, tt=tt, first=(tt == 0), prev=gate_prev):
                if first:
                    t.wait_ge(self.gi.h, gi_c)
                    t.wait_ge(dv.h, ln2_dv)
                if prev is not None:
                    t.wait_ge(dv.h, prev)
                for ic in range(NIC):
                    t.matmul(self.pg0[:, 0:256],
                             xh2[:, ic, tt * 128:(tt + 1) * 128],
                             big0[:, OF_WQG + ic * 256:OF_WQG + (ic + 1) * 256],
                             start=(ic == 0), stop=(ic == NIC - 1))
                for ic in range(NIC):
                    ins = t.matmul(self.pg1[:, 0:256],
                                   xh2[:, ic, tt * 128:(tt + 1) * 128],
                                   big0[:, OF_WKG + ic * 256:OF_WKG + (ic + 1) * 256],
                                   start=(ic == 0), stop=(ic == NIC - 1))
                pe.inc(ins)

            self.P(pq)
            pe_qk = pe.n

            def vg(v, pe_qk=pe_qk):
                v.wait_ge(pe.h, pe_qk)
                dv.inc(v.tensor_copy(gbuf[:, 0, :], self.pg0[:, 0:256]))
                v.wait_ge(dv.h, dv.n)
                dv.inc(v.tensor_tensor(gbuf[:, 1, :], self.pg1[:, 0:256],
                                       gbuf[:, 0, :], op=ALU.mult))
                v.wait_ge(dv.h, dv.n)
                dv.inc(v.tensor_reduce(gbuf[:, 2, 0:8],
                                       gbuf[:, 1, :].rearrange("p (e d) -> p e d",
                                                               d=DK),
                                       axis=AX.X, op=ALU.add))
                v.wait_ge(dv.h, dv.n)
                dv.inc(v.reduce_max(self.stat[:, 8:9], gbuf[:, 2, 0:8], axis=AX.X,
                                    negate=True))
                v.wait_ge(dv.h, dv.n)
                dv.inc(v.tensor_scalar_mul(self.stat[:, 9:10], self.stat[:, 8:9],
                                           GSC))

            self.V(vg)
            dv_sc = dv.n

            def sg(s, dv_sc=dv_sc):
                s.wait_ge(dv.h, dv_sc)
                ac.inc(s.activation(gbuf[:, 3, 0:8], gbuf[:, 2, 0:8], AF.Exp,
                                    bias=self.stat[:, 9:10], scale=GSC,
                                    accum_out=self.stat[:, 10:11]))

            self.S(sg)
            ac_e = ac.n

            def vg2(v, ac_e=ac_e):
                v.wait_ge(ac.h, ac_e)
                dv.inc(v.reciprocal(self.stat[:, 11:12], self.stat[:, 10:11]))
                v.wait_ge(dv.h, dv.n)
                dv.inc(v.tensor_scalar(gbuf[:, 4, 0:8], gbuf[:, 3, 0:8],
                                       self.stat[:, 11:12], None, op0=ALU.mult))

            self.V(vg2)
            gate_prev = dv.n

            def ph(t, tt=tt, first=(tt == 0)):
                if first:
                    t.wait_ge(dv.h, ln2_dv)
                for ic in range(NIC):
                    ins = t.matmul(self.pg2[:, 0:128],
                                   xh2[:, ic, tt * 128:(tt + 1) * 128],
                                   big0[:, OF_AT + ic * 128:OF_AT + (ic + 1) * 128],
                                   start=(ic == 0), stop=(ic == NIC - 1))
                pe.inc(ins)

            self.P(ph)
            pe_h = pe.n

            def vh(v, pe_h=pe_h, wg_dv=gate_prev):
                v.wait_ge(pe.h, pe_h)
                v.wait_ge(dv.h, wg_dv)
                wv = gbuf[:, 4, 0:8]
                dv.inc(v.tensor_tensor(
                    gbuf[:, 1, 0:128].rearrange("p (e r) -> p e r", r=R_),
                    self.pg2[:, 0:128].rearrange("p (e r) -> p e r", r=R_),
                    wv.broadcast_to([128, 8, R_]), op=ALU.mult))

            self.V(vh)
            dv_g = dv.n

            def pt_(t, dv_g=dv_g, prev=gt_prev, first=(tt == 0)):
                if first:
                    t.wait_ge(dv.h, self.d0_done_dv)
                t.wait_ge(dv.h, dv_g)
                if prev is not None:
                    t.wait_ge(dv.h, prev)
                pe.inc(t.transpose(self.psA[:, 0:128], gbuf[:, 1, 0:128],
                                   self.ident[:]))

            self.P(pt_)
            pe_gt = pe.n

            def vgt(v, tt=tt, pe_gt=pe_gt):
                v.wait_ge(pe.h, pe_gt)
                dv.inc(v.tensor_copy(big0[:, OF_GT + tt * 128:OF_GT + (tt + 1) * 128],
                                     self.psA[:, 0:128]))

            self.V(vgt)
            gt_prev = dv.n
        gates_dv = self.dv.n

        gsems = (g0, g1, self.g2, self.g3, self.g4)
        nslot = 6
        ssem = lambda hg: gsems[(hg % nslot) % 5]
        half_cons = {}
        half_ld = {}
        bank_evac = {}
        xslot_cons = {}
        st_state = {}
        it = 0
        for ob in range(8):
            for qf in range(4):
                hg = ob * 4 + qf
                gc = ssem(hg)

                def gw(g, ob=ob, qf=qf, hg=hg, gc=gc,
                       cons=half_cons.get(hg - nslot)):
                    if cons is not None:
                        g.wait_ge(pe.h, cons)
                    if hg == 0:
                        g.wait_ge(dv.h, ln2_dv)
                    self._dma(gc, g).inc(g.dma_start(
                        out=wtslab[:, hg % nslot, :, :].rearrange("p a b -> p (a b)"),
                        in_=self.wt_dev[ob, qf]), 16)

                self.G(gw)
                half_ld[hg] = gc.n

            for tt in range(8):
                par = it % 2
                bank = (self.pvA, self.pvB)[par]
                xsl = (x0, x1)[par]

                def yx(y, ob=ob, tt=tt, par=par, xsl=xsl,
                       cons=xslot_cons.get(par), first=(it < 2)):
                    if first:
                        y.wait_ge(self.so.h, x2_so)
                        y.wait_ge(dv.h, ln2_dv)
                    if cons is not None:
                        y.wait_ge(dv.h, cons)
                    self._dma(xsl, y).inc(y.dma_start(
                        out=xst[:, par, :],
                        in_=self.x2_dram[tt * 128:(tt + 1) * 128,
                                         ob * 512:(ob + 1) * 512]), 16)

                self.Y(yx)
                x_ld = xsl.n

                def vxb(v, ob=ob, par=par, x_ld=x_ld, xsl=xsl, gi_bb=gi_bb):
                    v.wait_ge(xsl.h, x_ld)
                    v.wait_ge(self.gi.h, gi_bb)
                    dv.inc(v.tensor_tensor(xst[:, par, :], xst[:, par, :],
                                           bb[:, ob * 512:(ob + 1) * 512],
                                           op=ALU.add))

                self.V(vxb)
                dv_xb = dv.n

                def p(t, ob=ob, tt=tt, bank=bank,
                      ws=tuple((ob * 4 + q, half_ld[ob * 4 + q]) for q in range(4)),
                      first=(it == 0), ev=bank_evac.get(par)):
                    if first:
                        t.wait_ge(dv.h, gates_dv)
                    for hg, ld in ws:
                        t.wait_ge(ssem(hg).h, ld)
                    if ev is not None:
                        t.wait_ge(dv.h, ev)
                    for ic in range(NIC):
                        t.matmul(bank[:],
                                 xh2[:, ic, tt * 128:(tt + 1) * 128],
                                 wtslab[:, (ob * 4 + ic // 8) % nslot, ic % 8, :],
                                 start=(ic == 0), stop=False)
                    pe.inc(t.matmul(bank[:],
                                    big0[:, OF_GT + tt * 128:OF_GT + (tt + 1) * 128],
                                    big0[:, OF_BM + ob * 512:OF_BM + (ob + 1) * 512],
                                    start=False, stop=True))

                self.P(p)
                pe_mm = pe.n
                for q in range(4):
                    half_cons[ob * 4 + q] = pe.n

                def vev(v, par=par, bank=bank, pe_mm=pe_mm, dv_xb=dv_xb,
                        stprev=st_state.get(par)):
                    v.wait_ge(pe.h, pe_mm)
                    v.wait_ge(dv.h, dv_xb)
                    if stprev is not None:
                        v.wait_ge(self.so.h, stprev)
                    dv.inc(v.tensor_tensor(x3st[:, par, :], bank[:], xst[:, par, :],
                                           op=ALU.add))

                self.V(vev)
                bank_evac[par] = dv.n
                xslot_cons[par] = dv.n
                dv_x3 = dv.n

                def yo(y, ob=ob, tt=tt, par=par, dv_x3=dv_x3):
                    y.wait_ge(dv.h, dv_x3)
                    self._dma(self.so, y).inc(y.dma_start(
                        out=self.out[tt * 128:(tt + 1) * 128,
                                     ob * 512:(ob + 1) * 512],
                        in_=x3st[:, par, :]), 16)

                self.Y(yo)
                st_state[par] = self.so.n
                it += 1

    def emit_tail(self):
        total = self.so.n

        def y(y):
            y.wait_ge(self.so.h, total)

        self.Y(y)


_NC_CACHE = {}


def _get_nc(dump=()):
    key = tuple(sorted(dump))
    if key not in _NC_CACHE:
        _NC_CACHE[key] = KB(dump).build()
    return _NC_CACHE[key]


def _tables():
    inv = 1.0 / (500000.0 ** (np.arange(0, 128, 2, dtype=np.float64) / 128.0))
    wavelen = 2.0 * np.pi / inv
    low_wl, high_wl = 8192.0 / 1.0, 8192.0 / 4.0
    inv_scaled = np.where(wavelen > low_wl, inv / 8.0, inv)
    smooth = (8192.0 / wavelen - 1.0) / (4.0 - 1.0)
    smoothed = (1.0 - smooth) / 8.0 * inv + smooth * inv
    is_medium = (wavelen <= low_wl) & (wavelen >= high_wl)
    invf = np.where(is_medium, smoothed, inv_scaled)
    pos = np.arange(T, dtype=np.float64)
    ang = invf[:, None] * pos[None, :]
    cos = np.cos(ang)
    sin = np.sin(ang)
    cosT = np.concatenate([cos, cos], 0).astype(np.float32)
    sinS = np.concatenate([-sin, sin], 0).astype(np.float32)
    p = np.arange(128)[:, None]
    f = np.arange(512)[None, :]
    blocks = [np.where(cc * 128 + p <= f, 0.0, NEG).astype(np.float32)
              for cc in range(4)]
    masksT = np.concatenate(blocks, axis=1)
    ident = np.eye(128, dtype=np.float32)
    return cosT, sinS, masksT, ident


def kernel(L0_attn_x, L0_attn_Wq, L0_attn_Wk, L0_attn_Wv, L0_attn_Wo,
           L0_ffn_W, L0_ffn_b, L0_ffn_Wq, L0_ffn_Wk, L0_ffn_A, L0_ffn_Bm,
           _dump=(), _nc=None):
    A = lambda a: np.ascontiguousarray(np.asarray(a, dtype=np.float32))
    x = A(L0_attn_x)
    wq, wk, wv, wo = A(L0_attn_Wq), A(L0_attn_Wk), A(L0_attn_Wv), A(L0_attn_Wo)
    w, bf = A(L0_ffn_W), A(L0_ffn_b)
    wqg, wkg = A(L0_ffn_Wq), A(L0_ffn_Wk)
    a_, bm = A(L0_ffn_A), A(L0_ffn_Bm)

    cosT, sinS, masksT, ident = _tables()
    wT = A(w.T.reshape(4, 8, 128, 8, 512).transpose(3, 0, 2, 1, 4)
           .reshape(8, 4, 128, 8 * 512))
    wqgT = A(wqg.T.reshape(NIC, 128, 256).transpose(1, 0, 2).reshape(128, NIC * 256))
    wkgT = A(wkg.T.reshape(NIC, 128, 256).transpose(1, 0, 2).reshape(128, NIC * 256))
    aT = A(a_.transpose(2, 0, 1).reshape(NIC, 128, 128)
           .transpose(1, 0, 2).reshape(128, NIC * 128))
    bmT = A(bm.transpose(0, 2, 1).reshape(E_ * R_, D))
    brow = A(bf.reshape(1, D))

    nc = _nc if _nc is not None else _get_nc(_dump)
    in_maps = []
    for core in range(8):
        b, r = core // 4, core % 4
        in_maps.append({
            "x": A(x[b, TOK * r:TOK * (r + 1), :]),
            "wqT": A(wq.T[:, 1024 * r:1024 * (r + 1)]
                     .reshape(NIC, 128, QH, 128).transpose(2, 1, 0, 3)
                     .reshape(QH, 128, NIC * 128)),
            "wkT": A(wk.T[:, 256 * r:256 * (r + 1)]
                     .reshape(NIC, 128, KVH, 128).transpose(2, 1, 0, 3)
                     .reshape(KVH, 128, NIC * 128)),
            "wvT": A(wv.T[:, 256 * r:256 * (r + 1)]
                     .reshape(NIC, 128, 256).transpose(1, 0, 2)
                     .reshape(128, NIC * 256)),
            "woT": A(wo.T[1024 * r:1024 * (r + 1), :]
                     .reshape(8, 128, 8, 512).transpose(2, 1, 0, 3)
                     .reshape(8, 128, 8 * 512)),
            "wT": wT,
            "bffn": brow,
            "wqgT": wqgT,
            "wkgT": wkgT,
            "aT": aT,
            "bmT": bmT,
            "cosT": cosT,
            "sinS": sinS,
            "masksT": masksT,
            "ident": ident,
        })

    res = run_bass_kernel_spmd(nc, in_maps, core_ids=list(range(8)))
    out = np.empty((B, T, D), np.float32)
    for core in range(8):
        b, r = core // 4, core % 4
        out[b, TOK * r:TOK * (r + 1), :] = res.results[core]["out"]
    if _dump:
        return out, res
    return out


# version 14 (final)
# speedup vs baseline: 1.0395x; 1.0085x over previous
import math
import numpy as np
import ml_dtypes
from contextlib import ExitStack

import concourse.bass as bass
import concourse.mybir as mybir
from concourse.bass_utils import run_bass_kernel_spmd

F32 = mybir.dt.float32
BF16 = mybir.dt.bfloat16
AF = mybir.ActivationFunctionType
ALU = mybir.AluOpType
AX = mybir.AxisListType

B, T, D = 2, 4096, 4096
NH, NKV, HD = 32, 8, 128
E_, DK, R_ = 8, 32, 16
TOK = 1024
NIC = D // 128
QH, KVH = 8, 2
SCALE = 1.0 / math.sqrt(HD)
SHIFT = 10.0
GSC = 1.0 / math.sqrt(DK)
EPS = 1e-5
NEG = -1.0e9

OF_WQG, OF_WKG, OF_AT, OF_BM, OF_GT = 0, 8192, 16384, 20480, 24576


class Ctr:
    def __init__(self, stack, nc, name):
        self.h = stack.enter_context(nc.semaphore(name))
        self.n = 0

    def inc(self, ins, k=1):
        ins.then_inc(self.h, k)
        self.n += k
        return self.n


class KB:
    def __init__(self, dump=(), stop="FFN"):
        self.dump = set(dump)
        self.stop = stop
        self.nc = bass.Bass("TRN2", target_bir_lowering=False)

    def _dma(self, ctr, eng):
        eng.wait_ge(ctr.h, ctr.n)
        return ctr

    def P(self, f): self.blk.tensor(f)
    def V(self, f): self.blk.vector(f)
    def S(self, f): self.blk.scalar(f)
    def G(self, f): self.blk.gpsimd(f)
    def Y(self, f): self.blk.sync(f)

    def build(self):
        nc = self.nc
        st = self.stack = ExitStack()
        st.__enter__()

        di = lambda n, shp, dt=F32: nc.dram_tensor(n, shp, dt, kind="ExternalInput").ap()
        self.x_in = di("x", [TOK, D])
        self.wq_dev = di("wqT", [QH, 128, NIC * 128])
        self.wk_dev = di("wkT", [KVH, 128, NIC * 128])
        self.wv_dev = di("wvT", [128, NIC * 256])
        self.wo_dev = di("woT", [8, 128, 8 * 512])
        self.wt_dev = di("wT", [8, 4, 128, 8 * 512])
        self.b_in = di("bffn", [1, D])
        self.wqg_dev = di("wqgT", [128, NIC * 256])
        self.wkg_dev = di("wkgT", [128, NIC * 256])
        self.at_dev = di("aT", [128, NIC * 128])
        self.bmT_in = di("bmT", [E_ * R_, D])
        self.cosT = di("cosT", [128, T])
        self.sinS = di("sinS", [128, T])
        self.masksT_in = di("masksT", [128, 4 * 512])
        self.ident_in = di("ident", [128, 128])

        self.out = nc.dram_tensor("out", [TOK, D], F32, kind="ExternalOutput").ap()

        dint = lambda n, shp, dt: nc.dram_tensor(n, shp, dt, kind="Internal").ap()
        self.xh_cc = [dint("xh_cc0", [D, TOK // 2], BF16),
                      dint("xh_cc1", [D, TOK // 2], BF16)]
        self.xh_all = [dint("xh_all0", [4 * D, TOK // 2], BF16),
                       dint("xh_all1", [4 * D, TOK // 2], BF16)]
        self.pw_dram = [dint(f"pw_dram{j}", [T, D // 4], BF16) for j in range(4)]
        self.x2a_dram = [dint(f"x2a_dram{j}", [TOK, D // 4], BF16) for j in range(4)]
        self.x2_dram = dint("x2_dram", [TOK, D], F32)

        self.dumps = {}
        for name, shp, dt in (
            ("d_xh", [4 * D, TOK], BF16),
            ("d_q", [128, QH * T], BF16),
            ("d_k", [128, KVH * T], BF16),
            ("d_v", [128, KVH * T], BF16),
            ("d_ao", [128, QH * T], BF16),
            ("d_x2", [TOK, D], F32),
            ("d_x2a", [TOK, D], BF16),
            ("d_st", [128, 64], F32),
        ):
            if name in self.dump:
                self.dumps[name] = nc.dram_tensor(name, shp, dt, kind="ExternalOutput").ap()

        sb = lambda n, shp, dt: st.enter_context(nc.sbuf_tensor(n, shp, dt))
        self.big0 = sb("big0", [128, 32768], BF16)
        self.big1 = sb("big1", [128, 32768], BF16)
        self.stg = sb("stg", [128, 2, 512], F32)
        self.ident = sb("ident_sb", [128, 128], F32)
        self.ident_bf = sb("ident_bf", [128, 128], BF16)
        self.ones_c = sb("ones_c", [128, 1], BF16)
        self.ones_r = sb("ones_r", [1, 128], F32)
        self.stat = sb("stat", [128, 16], F32)
        self.lrow = sb("lrow", [1, 512], F32)
        self.csts = sb("csts", [128, 2], F32)
        nc.const_aps.register(F32, EPS, self.csts[:, 0:1])
        nc.const_aps.register(F32, -SHIFT, self.csts[:, 1:2])

        ps = lambda n, shp: st.enter_context(nc.psum_tensor(n, shp, F32))
        self.sT2 = ps("sT2", [128, 1024])
        self.psA = self.sT2[:, 0:512]
        self.psB = self.sT2[:, 512:1024]
        self.pvA = ps("pvA", [128, 512])
        self.pvB = ps("pvB", [128, 512])
        self.psl = ps("psl", [128, 512])
        self.pg0 = ps("pg0", [128, 512])
        self.pg1 = ps("pg1", [128, 512])
        self.pg2 = ps("pg2", [128, 512])

        C = lambda n: Ctr(st, nc, n)
        self.pe, self.dv, self.ac = C("pe"), C("dv"), C("ac")
        self.g0, self.g1, self.g2, self.gi = C("g0"), C("g1"), C("g2"), C("gi")
        self.g3, self.g4 = C("g3"), C("g4")
        self.x0, self.x1, self.xq, self.xi = C("x0"), C("x1"), C("xq"), C("xi")
        self.so = C("so")
        self.cc1, self.cc2 = C("cc1"), C("cc2")

        self.blk = st.enter_context(nc.Block())

        self.emit_init()
        with nc.sbuf_tensor("xpair", [128, 3, 4096], F32) as xpair, \
             nc.sbuf_tensor("sqsink", [128, 512], F32) as sink:
            self.emit_ln_transpose(xpair, None, sink, src="x", dst_big=self.big0)
        self.emit_gather1()
        if self.stop == "A":
            self.emit_tail()
            st.__exit__(None, None, None)
            return self.nc
        with nc.sbuf_tensor("kT", [128, KVH * T], BF16) as kT_sb, \
             nc.sbuf_tensor("vsb", [128, KVH * T], BF16) as v_sb:
            with nc.sbuf_tensor("wslab", [128, 2, 4096], BF16) as wslab, \
                 nc.sbuf_tensor("wvsb", [128, NIC * 256], BF16) as wv_sb, \
                 nc.sbuf_tensor("cs", [128, 2, 512], F32) as cos_sb, \
                 nc.sbuf_tensor("sn", [128, 2, 512], F32) as sin_sb:
                self.emit_qkv(kT_sb, v_sb, wslab, wv_sb, cos_sb, sin_sb)
            if self.stop != "QKV":
                with nc.sbuf_tensor("masks", [128, 4 * 512], BF16) as mask_sb, \
                     nc.sbuf_tensor("ptc", [128, 6, 512], BF16) as ptc, \
                     nc.sbuf_tensor("rbst", [128, 512], F32) as rbst:
                    self.emit_attention(kT_sb, v_sb, mask_sb, ptc, rbst)
        if self.stop not in ("QKV", "ATT"):
            with nc.sbuf_tensor("woslab", [128, 2, 8, 512], BF16) as woslab, \
                 nc.sbuf_tensor("pwst", [128, 2, 512], BF16) as pwst:
                self.emit_wo_partial(woslab, pwst)
            self.emit_scatter2()
            with nc.sbuf_tensor("x2t", [128, 3, 4096], F32) as x2t, \
                 nc.sbuf_tensor("x2a", [128, 2, 4096], BF16) as x2a, \
                 nc.sbuf_tensor("sqsink2", [128, 512], F32) as sink2:
                self.emit_ln_transpose(x2t, x2a, sink2, src="x2", dst_big=self.big1)
            if self.stop != "LN2":
                with nc.sbuf_tensor("wtslab", [128, 6, 8, 512], BF16) as wtslab, \
                     nc.sbuf_tensor("bb", [128, 4096], BF16) as bb, \
                     nc.sbuf_tensor("gbuf", [128, 5, 256], F32) as gbuf, \
                     nc.sbuf_tensor("x3st", [128, 2, 512], F32) as x3st:
                    self.emit_ffn(wtslab, bb, gbuf, self.stg, x3st)
        self.emit_tail()

        st.__exit__(None, None, None)
        return self.nc

    def emit_init(self):
        def g(g):
            self._dma(self.gi, g).inc(g.dma_start(out=self.ident[:], in_=self.ident_in), 16)
            self._dma(self.gi, g).inc(g.dma_start(out=self.ident_bf[:], in_=self.ident_in), 16)

        self.G(g)

        def v(v):
            v.memset(self.ones_c[:], 1.0)
            v.memset(self.csts[:, 0:1], EPS)
            v.memset(self.csts[:, 1:2], -SHIFT)
            self.dv.inc(v.memset(self.ones_r[:], 1.0))

        self.V(v)
        self.init_gi = self.gi.n

    def emit_ln_transpose(self, xpair, x2a, sink, src, dst_big):
        pe, dv, ac = self.pe, self.dv, self.ac
        x0, x1 = self.x0, self.x1
        bank_evac = {}
        slot_free = {}
        x2a_free = None

        for tt in range(8):
            slot = tt % 3
            xc = (x0, x1)[tt % 2]
            xtile = xpair[:, slot, :]

            def y(y, tt=tt, xc=xc, xtile=xtile, cons=slot_free.get(slot),
                  x2a_free=x2a_free):
                if src == "x2" and tt < 3:
                    y.wait_ge(self.cc2.h, 4)
                if cons is not None:
                    y.wait_ge(pe.h, cons)
                self._dma(xc, y).inc(y.dma_start(out=xtile,
                                   in_=self.x_in[tt * 128:(tt + 1) * 128, :]), 16)
                if src == "x2":
                    if x2a_free is not None:
                        y.wait_ge(dv.h, x2a_free)
                    for j in range(4):
                        self._dma(xc, y).inc(y.dma_start(
                            out=x2a[:, tt % 2, j * 1024:(j + 1) * 1024],
                            in_=self.x2a_dram[j][tt * 128:(tt + 1) * 128, :]), 16)

            self.Y(y)
            ldv = xc.n

            y2_so = None
            if src == "x2":
                def va(v, xtile=xtile, ldv=ldv, xc=xc, tt=tt):
                    v.wait_ge(xc.h, ldv)
                    dv.inc(v.tensor_tensor(xtile, xtile, x2a[:, tt % 2, :],
                                           op=ALU.add))

                self.V(va)
                dv_x2 = dv.n
                if tt >= 1:
                    x2a_free = self._x2a_prev
                self._x2a_prev = dv.n

                def y2(y, tt=tt, xtile=xtile, dv_x2=dv_x2):
                    y.wait_ge(dv.h, dv_x2)
                    self._dma(self.so, y).inc(y.dma_start(
                        out=self.x2_dram[tt * 128:(tt + 1) * 128, :], in_=xtile), 16)

                self.Y(y2)
                y2_so = self.so.n
                ready = ("dv", dv_x2)
            else:
                ready = ("xc", ldv)

            def v1(v, xtile=xtile, ready=ready, xc=xc):
                if ready[0] == "xc":
                    v.wait_ge(xc.h, ready[1])
                else:
                    v.wait_ge(dv.h, ready[1])
                dv.inc(v.reduce_sum(self.stat[:, 0:1], xtile, axis=AX.X))

            self.V(v1)
            dv_s1 = dv.n
            prev_var = getattr(self, "_lnt_prev_xhat", None)

            def s1(s, xtile=xtile, ready=ready, xc=xc, prev_var=prev_var):
                if ready[0] == "xc":
                    s.wait_ge(xc.h, ready[1])
                else:
                    s.wait_ge(dv.h, ready[1])
                if prev_var is not None:
                    s.wait_ge(dv.h, prev_var)
                for c in range(8):
                    ac.inc(s.activation(sink[:], xtile[:, c * 512:(c + 1) * 512],
                                        AF.Square, bias=0.0, scale=1.0,
                                        accum_out=self.stat[:, 8 + c:9 + c]))
                    if c < 7:
                        s.wait_ge(ac.h, ac.n)

            self.S(s1)
            ac_sq = ac.n

            def v2(v, ac_sq=ac_sq, dv_s1=dv_s1):
                v.wait_ge(ac.h, ac_sq)
                v.wait_ge(dv.h, dv_s1)
                v.reduce_sum(self.stat[:, 1:2], self.stat[:, 8:16], axis=AX.X)
                dv.inc(v.tensor_scalar_mul(self.stat[:, 2:3], self.stat[:, 0:1],
                                           -1.0 / D))
                v.wait_ge(dv.h, dv.n)
                dv.inc(v.tensor_mul(self.stat[:, 3:4], self.stat[:, 2:3],
                                    self.stat[:, 2:3]))
                v.wait_ge(dv.h, dv.n)
                dv.inc(v.tensor_scalar(self.stat[:, 4:5], self.stat[:, 1:2],
                                       1.0 / D, self.stat[:, 3:4],
                                       op0=ALU.mult, op1=ALU.subtract))

            self.V(v2)
            dv_var = dv.n

            def s2(s, dv_var=dv_var):
                s.wait_ge(dv.h, dv_var)
                ac.inc(s.activation(self.stat[:, 5:6], self.stat[:, 4:5], AF.Sqrt,
                                    bias=EPS, scale=1.0))

            self.S(s2)
            ac_std = ac.n

            def v3(v, ac_std=ac_std, xtile=xtile, y2_so=y2_so):
                v.wait_ge(ac.h, ac_std)
                if y2_so is not None:
                    v.wait_ge(self.so.h, y2_so)
                dv.inc(v.reciprocal(self.stat[:, 6:7], self.stat[:, 5:6]))
                v.wait_ge(dv.h, dv.n)
                dv.inc(v.tensor_scalar(xtile, xtile, self.stat[:, 2:3],
                                       self.stat[:, 6:7], op0=ALU.add, op1=ALU.mult))

            self.V(v3)
            dv_xhat = dv.n
            self._lnt_prev_xhat = dv_xhat

            for grp in range(8):
                par = grp % 2
                bank = (self.pg0, self.pg1)[par]

                def p(t, grp=grp, bank=bank, dv_xhat=dv_xhat, xtile=xtile,
                      prev=bank_evac.get(par), first=(tt == 0 and grp == 0)):
                    if first:
                        t.wait_ge(self.gi.h, self.init_gi)
                    t.wait_ge(dv.h, dv_xhat)
                    if prev is not None:
                        t.wait_ge(dv.h, prev)
                    for j in range(4):
                        ic = grp * 4 + j
                        ins = t.transpose(bank[:, j * 128:(j + 1) * 128],
                                          xtile[:, ic * 128:(ic + 1) * 128],
                                          self.ident[:])
                    pe.inc(ins)

                self.P(p)
                pe_tr = pe.n

                def v4(v, grp=grp, bank=bank, tt=tt, pe_tr=pe_tr):
                    v.wait_ge(pe.h, pe_tr)
                    o = dst_big[:].rearrange("p (ic t) -> p ic t", t=1024)
                    dv.inc(v.tensor_copy(
                        o[:, grp * 4:(grp + 1) * 4, tt * 128:(tt + 1) * 128],
                        bank[:].rearrange("p (j t) -> p j t", t=128)))

                self.V(v4)
                bank_evac[par] = dv.n
            slot_free[slot] = pe.n
            if src == "x" and tt == 3:
                self.ln1_half0_dv = self.dv.n
        if src == "x":
            self.ln1_done_dv = self.dv.n
        else:
            self.ln2_done_dv = self.dv.n
            self.x2_stores_so = self.so.n
            if "d_x2" in self.dumps:
                need = self.so.n

                def yd(y):
                    y.wait_ge(self.so.h, need)
                    self._dma(self.so, y).inc(y.dma_start(out=self.dumps["d_x2"],
                                            in_=self.x2_dram), 16)

                self.Y(yd)

    def emit_gather1(self):
        so = self.so
        needs = (self.ln1_half0_dv, self.ln1_done_dv)
        big = self.big0[:].rearrange("p (ic t) -> p ic t", t=1024)
        for hh in range(2):
            def y(y, hh=hh):
                y.wait_ge(self.dv.h, needs[hh])
                dst = self.xh_cc[hh].rearrange("(ic p) t -> p ic t", p=128)
                for h in range(4):
                    self._dma(so, y).inc(y.dma_start(
                        out=dst[:, h * 8:(h + 1) * 8, :],
                        in_=big[:, h * 8:(h + 1) * 8, hh * 512:(hh + 1) * 512]), 16)

            self.Y(y)
            need_so = so.n

            def g(g, hh=hh, need_so=need_so):
                g.wait_ge(so.h, need_so)
                self.cc1.inc(g.collective_compute(
                    "AllGather", ALU.bypass,
                    replica_groups=[[0, 1, 2, 3], [4, 5, 6, 7]],
                    ins=[self.xh_cc[hh]], outs=[self.xh_all[hh]]))

            self.G(g)

        if "d_xh" in self.dumps:
            def yd(y):
                y.wait_ge(self.cc1.h, 2)
                for hh in range(2):
                    self._dma(so, y).inc(y.dma_start(
                        out=self.dumps["d_xh"].rearrange(
                            "r (h t) -> r h t", h=2)[:, hh, :],
                        in_=self.xh_all[hh]), 16)
            self.Y(yd)

    def emit_qkv(self, kT_sb, v_sb, wslab, wv_sb, cos_sb, sin_sb):
        pe, dv, ac = self.pe, self.dv, self.ac
        g0, g1, xq, xi = self.g0, self.g1, self.xq, self.xi

        def gwv(g):
            g.wait_ge(dv.h, self.ln1_done_dv)
            self._dma(self.gi, g).inc(g.dma_start(out=wv_sb[:], in_=self.wv_dev), 16)

        self.G(gwv)
        gi_wv = self.gi.n

        qk_banks = (self.psA, self.psB, self.pg0[:], self.pg1[:])
        bank_evac = {}
        slab_cons = {}
        self._qkv_sli = 0
        self._qkv_bc = 0
        xq_cons = None
        rope_done = None

        for qr in range(4):
            xq_ld_h = [None, None]
            for hh in range(2):
                def yq(y, qr=qr, hh=hh, xq_cons=xq_cons):
                    if qr == 0:
                        y.wait_ge(self.cc1.h, hh + 1)
                    elif hh == 0:
                        y.wait_ge(pe.h, xq_cons)
                    dst = self.big0[:].rearrange("p (ic t) -> p ic t", t=1024)
                    src = self.xh_all[hh][qr * D:(qr + 1) * D, :].rearrange(
                        "(ic p) t -> p ic t", p=128)
                    for h in range(2):
                        self._dma(xq, y).inc(y.dma_start(
                            out=dst[:, h * 16:(h + 1) * 16,
                                    hh * 512:(hh + 1) * 512],
                            in_=src[:, h * 16:(h + 1) * 16, :]), 16)

                self.Y(yq)
                xq_ld_h[hh] = xq.n
            xq_ld = xq.n
            big = self.big0[:].rearrange("p (ic t) -> p ic t", t=1024)

            def ycs(y, qr=qr, rope_done=rope_done):
                y.wait_ge(dv.h, self.ln1_done_dv if rope_done is None else rope_done)
                for tb in range(2):
                    c0 = qr * 1024 + tb * 512
                    self._dma(xi, y).inc(y.dma_start(out=cos_sb[:, tb, :],
                                       in_=self.cosT[:, c0:c0 + 512]), 16)
                    self._dma(xi, y).inc(y.dma_start(out=sin_sb[:, tb, :],
                                       in_=self.sinS[:, c0:c0 + 512]), 16)

            self.Y(ycs)
            xi_cs = xi.n

            work = ([(oc, (0, 1)) for oc in range(QH + KVH)] if qr > 0 else
                    [(oc, (0,)) for oc in range(QH + KVH)] +
                    [(oc, (1,)) for oc in range(QH + KVH)])
            for oc, tbs in work:
                is_q = oc < QH
                wsrc = self.wq_dev[oc] if is_q else self.wk_dev[oc - QH]
                slot = self._qkv_sli % 2
                self._qkv_sli += 1
                gc = (g0, g1)[slot]

                def gw(g, wsrc=wsrc, slot=slot, gc=gc,
                       cons=slab_cons.get(slot)):
                    if cons is not None:
                        g.wait_ge(pe.h, cons)
                    else:
                        g.wait_ge(dv.h, self.ln1_done_dv)
                    self._dma(gc, g).inc(g.dma_start(out=wslab[:, slot, :], in_=wsrc), 16)

                self.G(gw)
                gw_ld = gc.n

                for tb in tbs:
                    bidx = self._qkv_bc % 4
                    self._qkv_bc += 1
                    bank = qk_banks[bidx]

                    def p(t, tb=tb, bank=bank, slot=slot, gw_ld=gw_ld, gc=gc,
                          xq_ld=xq_ld_h[tb], ev=bank_evac.get(bidx)):
                        t.wait_ge(gc.h, gw_ld)
                        t.wait_ge(xq.h, xq_ld)
                        if ev is not None:
                            t.wait_ge(dv.h, ev)
                        for ic in range(NIC):
                            ins = t.matmul(bank,
                                           wslab[:, slot, ic * 128:(ic + 1) * 128],
                                           big[:, ic, tb * 512:(tb + 1) * 512],
                                           start=(ic == 0), stop=(ic == NIC - 1))
                        pe.inc(ins)

                    self.P(p)
                    pe_mm = pe.n
                    slab_cons[slot] = pe.n

                    def vr(v, oc=oc, tb=tb, qr=qr, bank=bank, pe_mm=pe_mm,
                           is_q=is_q, xi_cs=xi_cs, dvfence=dv.n):
                        v.wait_ge(pe.h, pe_mm)
                        v.wait_ge(xi.h, xi_cs)
                        v.wait_ge(dv.h, dvfence)
                        tmp = self.stg[:, 0, :]
                        t2 = self.stg[:, 1, :]
                        sn = sin_sb[:, tb, :]
                        cs = cos_sb[:, tb, :]
                        v.tensor_tensor(tmp[0:64, :], bank[64:128, :], sn[0:64, :],
                                        op=ALU.mult)
                        v.tensor_tensor(tmp[64:128, :], bank[0:64, :], sn[64:128, :],
                                        op=ALU.mult)
                        dv.inc(v.tensor_tensor(t2[:], bank, cs, op=ALU.mult))
                        v.wait_ge(dv.h, dv.n)
                        s0 = (oc * T if is_q else (oc - QH) * T) + qr * 1024 + tb * 512
                        dst = (self.big1 if is_q else kT_sb)[:, s0:s0 + 512]
                        dv.inc(v.tensor_tensor(dst, t2[:], tmp[:], op=ALU.add))

                    self.V(vr)
                    bank_evac[bidx] = dv.n

            rope_done = dv.n

            for tt in range(8):
                cg = qr * 8 + tt

                def pv(t, tt=tt, gi_wv=gi_wv, xq_ld=xq_ld_h[tt // 4],
                       ev=bank_evac.get("v")):
                    t.wait_ge(self.gi.h, gi_wv)
                    t.wait_ge(xq.h, xq_ld)
                    if ev is not None:
                        t.wait_ge(dv.h, ev)
                    for ic in range(NIC):
                        ins = t.matmul(self.pg2[:, 0:256],
                                       big[:, ic, tt * 128:(tt + 1) * 128],
                                       wv_sb[:, ic * 256:(ic + 1) * 256],
                                       start=(ic == 0), stop=(ic == NIC - 1))
                    pe.inc(ins)

                self.P(pv)
                pe_v = pe.n
                xq_cons = pe.n

                def vv(v, cg=cg, pe_v=pe_v):
                    v.wait_ge(pe.h, pe_v)
                    dst = v_sb[:].rearrange("p (kv c d) -> p kv c d", kv=KVH, d=128)
                    src = self.pg2[:, 0:256].rearrange("p (kv d) -> p kv d", d=128)
                    dv.inc(v.tensor_copy(dst[:, :, cg, :], src))

                self.V(vv)
                bank_evac["v"] = dv.n

        self.qkv_done_dv = self.dv.n

        for nm, buf in (("d_q", self.big1[:, 0:QH * T]), ("d_k", kT_sb[:]),
                        ("d_v", v_sb[:])):
            if nm in self.dumps:
                def yd(y, nm=nm, buf=buf):
                    y.wait_ge(dv.h, self.qkv_done_dv)
                    self._dma(self.so, y).inc(y.dma_start(out=self.dumps[nm], in_=buf), 16)
                self.Y(yd)

    def emit_attention(self, kT_sb, v_sb, mask_sb, ptc, rbst):
        pe, dv, ac = self.pe, self.dv, self.ac

        def gm(g):
            g.wait_ge(self.dv.h, self.qkv_done_dv)
            self._dma(self.gi, g).inc(g.dma_start(out=mask_sb[:], in_=self.masksT_in), 16)

        self.G(gm)
        gi_m = self.gi.n
        qkv_dv = self.qkv_done_dv

        sT_banks = (self.psA, self.psB)
        pv_banks = (self.pvA, self.pvB)
        l_banks = (self.psl, self.pg2)
        LROW = (0, 32, 64, 0)

        chunk_list = []
        groups = []
        for kvh in range(KVH):
            for qh4 in range(4):
                qh = kvh * 4 + qh4
                for qc in range(8):
                    g = len(groups)
                    n = 4 * (qc + 1)
                    for ci in range(n):
                        bk, cc = ci // 4, ci % 4
                        chunk_list.append((g, kvh, qh, qc, bk * 4 + cc,
                                           cc if bk == qc else None,
                                           ci == 0, ci == n - 1))
                    groups.append((kvh, qh, qc))
        NCH = len(chunk_list)
        NG = len(groups)

        ac0 = ac.n
        exp_ac = lambda k: ac0 + k + 1
        dv0 = dv.n
        vf_dv = lambda g: dv0 + 6 * g + 3
        rbcopy_dv = lambda g: dv0 + 6 * g + 4
        aomult_dv = lambda g: dv0 + 6 * g + 6

        pe_sT = [None] * NCH
        pe_slotfree = [None] * NCH
        pe_rb = [None] * NG
        pe_pv = [None] * NG
        first_k = {}
        for k, (g, *_rest) in enumerate(chunk_list):
            first_k.setdefault(g, k)

        def P(t):
            pending_rb = []

            def emit_sT(k):
                g, kvh, qh, qc, kc, diag_cc, first, last = chunk_list[k]
                if k == 0:
                    t.wait_ge(dv.h, qkv_dv)
                    t.wait_ge(self.gi.h, gi_m)
                if k >= 2:
                    t.wait_ge(ac.h, exp_ac(k - 2))
                if first and g >= 2:
                    t.wait_ge(dv.h, aomult_dv(g - 2))
                ins = t.matmul(
                    sT_banks[k % 2],
                    kT_sb[:, kvh * T + kc * 128:kvh * T + (kc + 1) * 128],
                    self.big1[:, qh * T + qc * 512:qh * T + qc * 512 + 512],
                    start=True, stop=(diag_cc is None))
                if diag_cc is not None:
                    ins = t.matmul(sT_banks[k % 2], self.ident_bf[:],
                                   mask_sb[:, diag_cc * 512:(diag_cc + 1) * 512],
                                   start=False, stop=True)
                pe.inc(ins)
                pe_sT[k] = pe.n

            def emit_p2(k):
                g, kvh, qh, qc, kc, diag_cc, first, last = chunk_list[k]
                ci = k - first_k[g]
                t.wait_ge(ac.h, exp_ac(k))
                pe.inc(t.matmul(
                    pv_banks[g % 2][:],
                    v_sb[:, kvh * T + kc * 128:kvh * T + (kc + 1) * 128],
                    ptc[:, k % 6, :], start=first, stop=last))
                pe_slotfree[k] = pe.n
                if ci % 4 == 3:
                    nb = qc + 1
                    b = ci // 4
                    lb = l_banks[g % 2]
                    for j in range(4):
                        kk = k - 3 + j
                        row = LROW[j]
                        is_start = (b == 0) and (j != 3)
                        is_stop = (b == nb - 1) and (j != 0)
                        pe.inc(t.matmul(
                            lb[row:row + 1, :], self.ones_c[:],
                            ptc[:, kk % 6, :],
                            start=is_start, stop=is_stop,
                            tile_position=(0, row)))
                        pe_slotfree[kk] = pe.n
                if last:
                    pe_pv[g] = pe.n
                    pending_rb.append((g, k + 4))

            def emit_rb(g):
                t.wait_ge(dv.h, vf_dv(g))
                if g >= 2:
                    t.wait_ge(dv.h, rbcopy_dv(g - 2))
                rb = (self.pg0, self.pg1)[g % 2]
                pe.inc(t.matmul(rb[:], self.ones_r[:], self.lrow[:],
                                start=True, stop=True))
                pe_rb[g] = pe.n

            for k in range(NCH):
                emit_sT(k)
                while pending_rb and pending_rb[0][1] <= k:
                    emit_rb(pending_rb.pop(0)[0])
                if k >= 2:
                    emit_p2(k - 2)
            emit_p2(NCH - 2)
            emit_p2(NCH - 1)
            while pending_rb:
                emit_rb(pending_rb.pop(0)[0])

        self.P(P)

        def V(v):
            for g, (kvh, qh, qc) in enumerate(groups):
                lb = l_banks[g % 2]
                v.wait_ge(pe.h, pe_pv[g])
                v.wait_ge(dv.h, dv.n)
                dv.inc(v.tensor_copy(rbst[0:1, :], lb[0:1, :]))
                v.wait_ge(dv.h, dv.n)
                dv.inc(v.tensor_tensor(rbst[0:1, :], rbst[0:1, :], lb[32:33, :],
                                       op=ALU.add))
                v.wait_ge(dv.h, dv.n)
                dv.inc(v.tensor_tensor(self.lrow[:], rbst[0:1, :], lb[64:65, :],
                                       op=ALU.add))
                assert dv.n == vf_dv(g)
                v.wait_ge(pe.h, pe_rb[g])
                v.wait_ge(dv.h, dv.n)
                rb = (self.pg0, self.pg1)[g % 2]
                dv.inc(v.tensor_copy(rbst[:], rb[:]))
                assert dv.n == rbcopy_dv(g)
                v.wait_ge(dv.h, dv.n)
                dv.inc(v.reciprocal(rbst[:], rbst[:]))
                v.wait_ge(dv.h, dv.n)
                dst = self.big0[:, qh * T + qc * 512:qh * T + qc * 512 + 512]
                dv.inc(v.tensor_tensor(dst, pv_banks[g % 2][:], rbst[:],
                                       op=ALU.mult))
                assert dv.n == aomult_dv(g)

        self.V(V)

        def S(s):
            for k in range(NCH):
                s.wait_ge(pe.h, pe_sT[k])
                if k >= 6:
                    s.wait_ge(pe.h, pe_slotfree[k - 6])
                ac.inc(s.activation(ptc[:, k % 6, :], sT_banks[k % 2],
                                    AF.Exp, bias=-SHIFT, scale=SCALE))
                assert ac.n == exp_ac(k)

        self.S(S)

        self.attn_done_dv = self.dv.n
        if "d_ao" in self.dumps:
            def yd(y):
                y.wait_ge(dv.h, self.attn_done_dv)
                self._dma(self.so, y).inc(y.dma_start(out=self.dumps["d_ao"],
                                        in_=self.big0[:, 0:QH * T]), 16)
            self.Y(yd)

    def emit_wo_partial(self, woslab, pwst):
        pe, dv = self.pe, self.dv
        g0, g1, x0, x1 = self.g0, self.g1, self.x0, self.x1
        attn_dv = self.attn_done_dv
        slab_cons = {}
        bank_evac = {}
        store_state = {}
        it = 0

        for fb in range(8):
            slot = fb % 2
            gc = (g0, g1)[slot]

            def gw(g, fb=fb, slot=slot, gc=gc, cons=slab_cons.get(slot)):
                if cons is not None:
                    g.wait_ge(pe.h, cons)
                else:
                    g.wait_ge(dv.h, attn_dv)
                self._dma(gc, g).inc(g.dma_start(
                    out=woslab[:, slot, :, :].rearrange("p a b -> p (a b)"),
                    in_=self.wo_dev[fb]), 16)

            self.G(gw)
            gw_ld = gc.n

            for tt in range(32):
                par = it % 2
                bank = (self.psA, self.psB)[par]

                def p(t, tt=tt, slot=slot, bank=bank, gw_ld=gw_ld, gc=gc,
                      first=(it == 0), ev=bank_evac.get(par)):
                    if first:
                        t.wait_ge(dv.h, attn_dv)
                    t.wait_ge(gc.h, gw_ld)
                    if ev is not None:
                        t.wait_ge(dv.h, ev)
                    for qh in range(QH):
                        ins = t.matmul(
                            bank[:],
                            self.big0[:, qh * T + tt * 128:qh * T + (tt + 1) * 128],
                            woslab[:, slot, qh, :],
                            start=(qh == 0), stop=(qh == QH - 1))
                    pe.inc(ins)

                self.P(p)
                pe_mm = pe.n
                slab_cons[slot] = pe.n

                def ve(v, par=par, bank=bank, pe_mm=pe_mm,
                       stfree=store_state.get(par)):
                    v.wait_ge(pe.h, pe_mm)
                    if stfree is not None:
                        v.wait_ge((x0, x1)[par].h, stfree)
                    dv.inc(v.tensor_copy(pwst[:, par, :], bank[:]))

                self.V(ve)
                bank_evac[par] = dv.n
                dv_e = dv.n

                def ys(y, fb=fb, tt=tt, par=par, dv_e=dv_e):
                    y.wait_ge(dv.h, dv_e)
                    self._dma((x0, x1)[par], y).inc(y.dma_start(
                        out=self.pw_dram[fb // 2][tt * 128:(tt + 1) * 128,
                                                  (fb % 2) * 512:(fb % 2) * 512 + 512],
                        in_=pwst[:, par, :]), 16)

                self.Y(ys)
                store_state[par] = (x0, x1)[par].n
                it += 1

            if fb % 2 == 1:
                j = fb // 2
                sx0, sx1 = self.x0.n, self.x1.n

                def g(g, j=j, sx0=sx0, sx1=sx1):
                    g.wait_ge(self.x0.h, sx0)
                    g.wait_ge(self.x1.h, sx1)
                    self.cc2.inc(g.collective_compute(
                        "ReduceScatter", ALU.add,
                        replica_groups=[[0, 1, 2, 3], [4, 5, 6, 7]],
                        ins=[self.pw_dram[j]], outs=[self.x2a_dram[j]]))

                self.G(g)

        self.wo_done_pe = self.pe.n
        self.d0_done_dv = self.dv.n

    def emit_scatter2(self):
        if "d_x2a" in self.dumps:
            def yd(y):
                y.wait_ge(self.cc2.h, 4)
                for j in range(4):
                    self._dma(self.so, y).inc(y.dma_start(
                        out=self.dumps["d_x2a"][:, j * 1024:(j + 1) * 1024],
                        in_=self.x2a_dram[j]), 16)
            self.Y(yd)

    def emit_ffn(self, wtslab, bb, gbuf, xst, x3st):
        pe, dv, ac = self.pe, self.dv, self.ac
        g0, g1, x0, x1, xi = self.g0, self.g1, self.x0, self.x1, self.xi
        ln2_dv = self.ln2_done_dv
        x2_so = self.x2_stores_so
        wo_pe = self.wo_done_pe
        big0, big1 = self.big0, self.big1

        def gco(g):
            g.wait_ge(pe.h, wo_pe)
            self._dma(self.gi, g).inc(g.dma_start(out=big0[:, OF_WQG:OF_WQG + 8192],
                                    in_=self.wqg_dev), 16)
            self._dma(self.gi, g).inc(g.dma_start(out=big0[:, OF_WKG:OF_WKG + 8192],
                                    in_=self.wkg_dev), 16)
            self._dma(self.gi, g).inc(g.dma_start(out=big0[:, OF_AT:OF_AT + 4096],
                                    in_=self.at_dev), 16)
            self._dma(self.gi, g).inc(g.dma_start(out=big0[:, OF_BM:OF_BM + 4096],
                                    in_=self.bmT_in), 16)

        self.G(gco)
        gi_c = self.gi.n

        def gbb(g):
            g.wait_ge(dv.h, ln2_dv)
            src = self.b_in.broadcast_to([1, D, 128]).transpose([2, 0, 1]) \
                .rearrange("p one d -> p (one d)")
            self._dma(self.gi, g).inc(g.dma_start(out=bb[:], in_=src), 16)

        self.G(gbb)
        gi_bb = self.gi.n

        xh2 = big1[:].rearrange("p (ic t) -> p ic t", t=1024)

        gate_prev = None
        gt_prev = None
        for tt in range(8):
            def pq(t, tt=tt, first=(tt == 0), prev=gate_prev):
                if first:
                    t.wait_ge(self.gi.h, gi_c)
                    t.wait_ge(dv.h, ln2_dv)
                if prev is not None:
                    t.wait_ge(dv.h, prev)
                for ic in range(NIC):
                    t.matmul(self.pg0[:, 0:256],
                             xh2[:, ic, tt * 128:(tt + 1) * 128],
                             big0[:, OF_WQG + ic * 256:OF_WQG + (ic + 1) * 256],
                             start=(ic == 0), stop=(ic == NIC - 1))
                for ic in range(NIC):
                    ins = t.matmul(self.pg1[:, 0:256],
                                   xh2[:, ic, tt * 128:(tt + 1) * 128],
                                   big0[:, OF_WKG + ic * 256:OF_WKG + (ic + 1) * 256],
                                   start=(ic == 0), stop=(ic == NIC - 1))
                pe.inc(ins)

            self.P(pq)
            pe_qk = pe.n

            def vg(v, pe_qk=pe_qk):
                v.wait_ge(pe.h, pe_qk)
                dv.inc(v.tensor_copy(gbuf[:, 0, :], self.pg0[:, 0:256]))
                v.wait_ge(dv.h, dv.n)
                dv.inc(v.tensor_tensor(gbuf[:, 1, :], self.pg1[:, 0:256],
                                       gbuf[:, 0, :], op=ALU.mult))
                v.wait_ge(dv.h, dv.n)
                dv.inc(v.tensor_reduce(gbuf[:, 2, 0:8],
                                       gbuf[:, 1, :].rearrange("p (e d) -> p e d",
                                                               d=DK),
                                       axis=AX.X, op=ALU.add))
                v.wait_ge(dv.h, dv.n)
                dv.inc(v.reduce_max(self.stat[:, 8:9], gbuf[:, 2, 0:8], axis=AX.X,
                                    negate=True))
                v.wait_ge(dv.h, dv.n)
                dv.inc(v.tensor_scalar_mul(self.stat[:, 9:10], self.stat[:, 8:9],
                                           GSC))

            self.V(vg)
            dv_sc = dv.n

            def sg(s, dv_sc=dv_sc):
                s.wait_ge(dv.h, dv_sc)
                ac.inc(s.activation(gbuf[:, 3, 0:8], gbuf[:, 2, 0:8], AF.Exp,
                                    bias=self.stat[:, 9:10], scale=GSC,
                                    accum_out=self.stat[:, 10:11]))

            self.S(sg)
            ac_e = ac.n

            def vg2(v, ac_e=ac_e):
                v.wait_ge(ac.h, ac_e)
                dv.inc(v.reciprocal(self.stat[:, 11:12], self.stat[:, 10:11]))
                v.wait_ge(dv.h, dv.n)
                dv.inc(v.tensor_scalar(gbuf[:, 4, 0:8], gbuf[:, 3, 0:8],
                                       self.stat[:, 11:12], None, op0=ALU.mult))

            self.V(vg2)
            gate_prev = dv.n

            def ph(t, tt=tt, first=(tt == 0)):
                if first:
                    t.wait_ge(dv.h, ln2_dv)
                for ic in range(NIC):
                    ins = t.matmul(self.pg2[:, 0:128],
                                   xh2[:, ic, tt * 128:(tt + 1) * 128],
                                   big0[:, OF_AT + ic * 128:OF_AT + (ic + 1) * 128],
                                   start=(ic == 0), stop=(ic == NIC - 1))
                pe.inc(ins)

            self.P(ph)
            pe_h = pe.n

            def vh(v, pe_h=pe_h, wg_dv=gate_prev):
                v.wait_ge(pe.h, pe_h)
                v.wait_ge(dv.h, wg_dv)
                wv = gbuf[:, 4, 0:8]
                dv.inc(v.tensor_tensor(
                    gbuf[:, 1, 0:128].rearrange("p (e r) -> p e r", r=R_),
                    self.pg2[:, 0:128].rearrange("p (e r) -> p e r", r=R_),
                    wv.broadcast_to([128, 8, R_]), op=ALU.mult))

            self.V(vh)
            dv_g = dv.n

            def pt_(t, dv_g=dv_g, prev=gt_prev, first=(tt == 0)):
                if first:
                    t.wait_ge(dv.h, self.d0_done_dv)
                t.wait_ge(dv.h, dv_g)
                if prev is not None:
                    t.wait_ge(dv.h, prev)
                pe.inc(t.transpose(self.psA[:, 0:128], gbuf[:, 1, 0:128],
                                   self.ident[:]))

            self.P(pt_)
            pe_gt = pe.n

            def vgt(v, tt=tt, pe_gt=pe_gt):
                v.wait_ge(pe.h, pe_gt)
                dv.inc(v.tensor_copy(big0[:, OF_GT + tt * 128:OF_GT + (tt + 1) * 128],
                                     self.psA[:, 0:128]))

            self.V(vgt)
            gt_prev = dv.n
        gates_dv = self.dv.n

        gsems = (g0, g1, self.g2, self.g3, self.g4)
        nslot = 6
        ssem = lambda hg: gsems[(hg % nslot) % 5]
        half_cons = {}
        half_ld = {}
        bank_evac = {}
        xslot_cons = {}
        st_state = {}
        it = 0
        for ob in range(8):
            for qf in range(4):
                hg = ob * 4 + qf
                gc = ssem(hg)

                def gw(g, ob=ob, qf=qf, hg=hg, gc=gc,
                       cons=half_cons.get(hg - nslot)):
                    if cons is not None:
                        g.wait_ge(pe.h, cons)
                    if hg == 0:
                        g.wait_ge(dv.h, ln2_dv)
                    self._dma(gc, g).inc(g.dma_start(
                        out=wtslab[:, hg % nslot, :, :].rearrange("p a b -> p (a b)"),
                        in_=self.wt_dev[ob, qf]), 16)

                self.G(gw)
                half_ld[hg] = gc.n

            for tt in range(8):
                par = it % 2
                bank = (self.pvA, self.pvB)[par]
                xsl = (x0, x1)[par]

                def yx(y, ob=ob, tt=tt, par=par, xsl=xsl,
                       cons=xslot_cons.get(par), first=(it < 2)):
                    if first:
                        y.wait_ge(self.so.h, x2_so)
                        y.wait_ge(dv.h, ln2_dv)
                    if cons is not None:
                        y.wait_ge(dv.h, cons)
                    self._dma(xsl, y).inc(y.dma_start(
                        out=xst[:, par, :],
                        in_=self.x2_dram[tt * 128:(tt + 1) * 128,
                                         ob * 512:(ob + 1) * 512]), 16)

                self.Y(yx)
                x_ld = xsl.n

                def vxb(v, ob=ob, par=par, x_ld=x_ld, xsl=xsl, gi_bb=gi_bb):
                    v.wait_ge(xsl.h, x_ld)
                    v.wait_ge(self.gi.h, gi_bb)
                    dv.inc(v.tensor_tensor(xst[:, par, :], xst[:, par, :],
                                           bb[:, ob * 512:(ob + 1) * 512],
                                           op=ALU.add))

                self.V(vxb)
                dv_xb = dv.n

                def p(t, ob=ob, tt=tt, bank=bank,
                      ws=tuple((ob * 4 + q, half_ld[ob * 4 + q]) for q in range(4)),
                      first=(it == 0), ev=bank_evac.get(par)):
                    if first:
                        t.wait_ge(dv.h, gates_dv)
                    if ev is not None:
                        t.wait_ge(dv.h, ev)
                    for ic in range(NIC):
                        if ic % 8 == 0:
                            hg, ld = ws[ic // 8]
                            t.wait_ge(ssem(hg).h, ld)
                        t.matmul(bank[:],
                                 xh2[:, ic, tt * 128:(tt + 1) * 128],
                                 wtslab[:, (ob * 4 + ic // 8) % nslot, ic % 8, :],
                                 start=(ic == 0), stop=False)
                    pe.inc(t.matmul(bank[:],
                                    big0[:, OF_GT + tt * 128:OF_GT + (tt + 1) * 128],
                                    big0[:, OF_BM + ob * 512:OF_BM + (ob + 1) * 512],
                                    start=False, stop=True))

                self.P(p)
                pe_mm = pe.n
                for q in range(4):
                    half_cons[ob * 4 + q] = pe.n

                def vev(v, par=par, bank=bank, pe_mm=pe_mm, dv_xb=dv_xb,
                        stprev=st_state.get(par)):
                    v.wait_ge(pe.h, pe_mm)
                    v.wait_ge(dv.h, dv_xb)
                    if stprev is not None:
                        v.wait_ge(self.so.h, stprev)
                    dv.inc(v.tensor_tensor(x3st[:, par, :], bank[:], xst[:, par, :],
                                           op=ALU.add))

                self.V(vev)
                bank_evac[par] = dv.n
                xslot_cons[par] = dv.n
                dv_x3 = dv.n

                def yo(y, ob=ob, tt=tt, par=par, dv_x3=dv_x3):
                    y.wait_ge(dv.h, dv_x3)
                    self._dma(self.so, y).inc(y.dma_start(
                        out=self.out[tt * 128:(tt + 1) * 128,
                                     ob * 512:(ob + 1) * 512],
                        in_=x3st[:, par, :]), 16)

                self.Y(yo)
                st_state[par] = self.so.n
                it += 1

    def emit_tail(self):
        total = self.so.n

        def y(y):
            y.wait_ge(self.so.h, total)

        self.Y(y)


_NC_CACHE = {}


def _get_nc(dump=()):
    key = tuple(sorted(dump))
    if key not in _NC_CACHE:
        _NC_CACHE[key] = KB(dump).build()
    return _NC_CACHE[key]


def _tables():
    inv = 1.0 / (500000.0 ** (np.arange(0, 128, 2, dtype=np.float64) / 128.0))
    wavelen = 2.0 * np.pi / inv
    low_wl, high_wl = 8192.0 / 1.0, 8192.0 / 4.0
    inv_scaled = np.where(wavelen > low_wl, inv / 8.0, inv)
    smooth = (8192.0 / wavelen - 1.0) / (4.0 - 1.0)
    smoothed = (1.0 - smooth) / 8.0 * inv + smooth * inv
    is_medium = (wavelen <= low_wl) & (wavelen >= high_wl)
    invf = np.where(is_medium, smoothed, inv_scaled)
    pos = np.arange(T, dtype=np.float64)
    ang = invf[:, None] * pos[None, :]
    cos = np.cos(ang)
    sin = np.sin(ang)
    cosT = np.concatenate([cos, cos], 0).astype(np.float32)
    sinS = np.concatenate([-sin, sin], 0).astype(np.float32)
    p = np.arange(128)[:, None]
    f = np.arange(512)[None, :]
    blocks = [np.where(cc * 128 + p <= f, 0.0, NEG).astype(np.float32)
              for cc in range(4)]
    masksT = np.concatenate(blocks, axis=1)
    ident = np.eye(128, dtype=np.float32)
    return cosT, sinS, masksT, ident


def kernel(L0_attn_x, L0_attn_Wq, L0_attn_Wk, L0_attn_Wv, L0_attn_Wo,
           L0_ffn_W, L0_ffn_b, L0_ffn_Wq, L0_ffn_Wk, L0_ffn_A, L0_ffn_Bm,
           _dump=(), _nc=None):
    A = lambda a: np.ascontiguousarray(np.asarray(a, dtype=np.float32))
    x = A(L0_attn_x)
    wq, wk, wv, wo = A(L0_attn_Wq), A(L0_attn_Wk), A(L0_attn_Wv), A(L0_attn_Wo)
    w, bf = A(L0_ffn_W), A(L0_ffn_b)
    wqg, wkg = A(L0_ffn_Wq), A(L0_ffn_Wk)
    a_, bm = A(L0_ffn_A), A(L0_ffn_Bm)

    cosT, sinS, masksT, ident = _tables()
    wT = A(w.T.reshape(4, 8, 128, 8, 512).transpose(3, 0, 2, 1, 4)
           .reshape(8, 4, 128, 8 * 512))
    wqgT = A(wqg.T.reshape(NIC, 128, 256).transpose(1, 0, 2).reshape(128, NIC * 256))
    wkgT = A(wkg.T.reshape(NIC, 128, 256).transpose(1, 0, 2).reshape(128, NIC * 256))
    aT = A(a_.transpose(2, 0, 1).reshape(NIC, 128, 128)
           .transpose(1, 0, 2).reshape(128, NIC * 128))
    bmT = A(bm.transpose(0, 2, 1).reshape(E_ * R_, D))
    brow = A(bf.reshape(1, D))

    nc = _nc if _nc is not None else _get_nc(_dump)
    in_maps = []
    for core in range(8):
        b, r = core // 4, core % 4
        in_maps.append({
            "x": A(x[b, TOK * r:TOK * (r + 1), :]),
            "wqT": A(wq.T[:, 1024 * r:1024 * (r + 1)]
                     .reshape(NIC, 128, QH, 128).transpose(2, 1, 0, 3)
                     .reshape(QH, 128, NIC * 128)),
            "wkT": A(wk.T[:, 256 * r:256 * (r + 1)]
                     .reshape(NIC, 128, KVH, 128).transpose(2, 1, 0, 3)
                     .reshape(KVH, 128, NIC * 128)),
            "wvT": A(wv.T[:, 256 * r:256 * (r + 1)]
                     .reshape(NIC, 128, 256).transpose(1, 0, 2)
                     .reshape(128, NIC * 256)),
            "woT": A(wo.T[1024 * r:1024 * (r + 1), :]
                     .reshape(8, 128, 8, 512).transpose(2, 1, 0, 3)
                     .reshape(8, 128, 8 * 512)),
            "wT": wT,
            "bffn": brow,
            "wqgT": wqgT,
            "wkgT": wkgT,
            "aT": aT,
            "bmT": bmT,
            "cosT": cosT,
            "sinS": sinS,
            "masksT": masksT,
            "ident": ident,
        })

    res = run_bass_kernel_spmd(nc, in_maps, core_ids=list(range(8)))
    out = np.empty((B, T, D), np.float32)
    for core in range(8):
        b, r = core // 4, core % 4
        out[b, TOK * r:TOK * (r + 1), :] = res.results[core]["out"]
    if _dump:
        return out, res
    return out
